# version 14 (final)
# speedup vs baseline: 1.3199x; 1.0296x over previous
from contextlib import ExitStack

import numpy as np
import ml_dtypes

import concourse.bass as bass
import concourse.mybir as mybir
from concourse.bass_utils import run_bass_kernel_spmd

F32 = mybir.dt.float32
BF16 = mybir.dt.bfloat16
I16 = mybir.dt.int16
I32 = mybir.dt.int32
I8 = mybir.dt.int8
AF = mybir.ActivationFunctionType
ALU = mybir.AluOpType
AX = mybir.AxisListType

NCORES = 8
B, S, D = 2, 8192, 1024
T = 2048
NT = T // 128
DCH = D // 128
DF = 512
NSC = 8
FFI = 3072
FCH = FFI // 128
EPS = 1e-5
QA = 32767.0
QW = 7.0
BF16NP = ml_dtypes.bfloat16


class Prog:

    ENGS = ("sync", "gpsimd", "vector", "scalar", "tensor")

    def __init__(self, name):
        self.nc = bass.Bass("TRN2", target_bir_lowering=False)
        self.name = name
        self.stages = []
        self._icnt = {e: 0 for e in self.ENGS}

    def chain(self, engine, eng, ins):
        ins.then_inc(self.intra[engine], 1)
        self._icnt[engine] += 1
        eng.wait_ge(self.intra[engine], self._icnt[engine])

    def stage(self, engine, fn, deps=None, dma=False):
        assert engine in self.ENGS
        if dma:
            assert engine in ("sync", "gpsimd")
        self.stages.append((engine, fn, dma, deps))
        return len(self.stages) - 1

    def build(self):
        nc = self.nc
        es = self._es
        NDS = 4
        prog = {e: es.enter_context(nc.semaphore(f"pg_{e}")) for e in self.ENGS}
        self.intra = {e: es.enter_context(nc.semaphore(f"in_{e}")) for e in self.ENGS}
        dsem = {e: [es.enter_context(nc.semaphore(f"dm_{e}{i}")) for i in range(NDS)]
                for e in ("sync", "gpsimd")}

        marker = []
        n_stage = {e: 0 for e in self.ENGS}
        n_dma_stage = {"sync": 0, "gpsimd": 0}
        n_dma = {("sync", i): 0 for i in range(NDS)}
        n_dma.update({("gpsimd", i): 0 for i in range(NDS)})
        slot_of = {}
        for sid, (engine, fn, dma, deps) in enumerate(self.stages):
            if dma:
                slot = n_dma_stage[engine] % NDS
                slot_of[sid] = slot
                n_dma_stage[engine] += 1
                nd = getattr(fn, "n_dma")
                n_dma[(engine, slot)] += nd
                marker.append(("dma", engine, slot, n_dma[(engine, slot)]))
            else:
                n_stage[engine] += 1
                marker.append(("prog", engine, None, n_stage[engine]))

        per_engine = {e: [] for e in self.ENGS}
        for sid, (engine, fn, dma, deps) in enumerate(self.stages):
            dep_ids = list(range(sid)) if deps is None else list(deps)
            waits = {}
            for d in dep_ids:
                kind, de, slot, cnt = marker[d]
                if de == engine and kind == "prog" and not dma:
                    continue
                if de == engine and kind == "dma" and dma:
                    continue
                key = (kind, de, slot)
                waits[key] = max(waits.get(key, 0), cnt)
            per_engine[engine].append((sid, fn, dma, waits))

        block = es.enter_context(nc.Block())

        def make_body(engine, items):
            def body(eng):
                for sid, fn, dma, waits in items:
                    if not dma and engine not in ("sync",):
                        pcnt = marker[sid][3] - 1
                        if pcnt >= 1:
                            eng.wait_ge(prog[engine], pcnt)
                    for (kind, de, slot), cnt in sorted(
                            waits.items(), key=lambda kv: (kv[0][0], kv[0][1], kv[0][2] or 0)):
                        if kind == "prog":
                            eng.wait_ge(prog[de], cnt)
                        else:
                            eng.wait_ge(dsem[de][slot], 16 * cnt)
                    if dma:
                        n = fn(eng, dsem[engine][slot_of[sid]])
                        assert n == fn.n_dma, (self.name, sid, n, fn.n_dma)
                    else:
                        last = fn(eng)
                        assert last is not None, (self.name, sid, "stage must return last ins")
                        last.then_inc(prog[engine], 1)
            return body

        for e in self.ENGS:
            if per_engine[e]:
                getattr(block, e)(make_body(e, per_engine[e]))
        self.prog_sems = prog
        self.dma_sems = dsem


def dma_stage(n):
    def deco(fn):
        fn.n_dma = n
        return fn
    return deco



def build_phase1():
    p = Prog("p1")
    nc = p.nc
    x = nc.dram_tensor("x", [T, D], F32, kind="ExternalInput").ap()
    wz = {}
    for nm in ("w_zf", "w_hf", "w_zs", "w_hs"):
        wz[nm] = nc.dram_tensor(nm, [D, DF], F32, kind="ExternalInput").ap()
    bz = {}
    for nm in ("b_zf", "b_hf", "b_zs", "b_hs"):
        bz[nm] = nc.dram_tensor(nm, [1, DF], F32, kind="ExternalInput").ap()
    identb = nc.dram_tensor("identb", [128, 128], BF16, kind="ExternalInput").ap()
    h_out = nc.dram_tensor("h_out", [D, T], F32, kind="ExternalOutput").ap()
    a_out = nc.dram_tensor("a_out", [D, T], F32, kind="ExternalOutput").ap()
    carr = nc.dram_tensor("carr", [2, D], F32, kind="ExternalOutput").ap()

    es = ExitStack()
    p._es = es
    es.__enter__()
    sb = lambda name, shape, dt: es.enter_context(nc.sbuf_tensor(name, shape, dt))
    pst = lambda name, shape, dt: es.enter_context(nc.psum_tensor(name, shape, dt))

    xs = sb("xs", [128, NT, D], F32)
    stat = sb("stat", [128, 5, NT], F32)
    sqs = sb("sqs", [128, D], F32)
    lnt = sb("lnt", [128, 2, D], BF16)
    xT = sb("xT", [128, DCH, T], BF16)
    wall = sb("wall", [128, DCH, 2048], BF16)
    ball = sb("ball", [128, 16], F32)
    nball = sb("nball", [128, 16], F32)
    zt = sb("zt", [128, 2, T], F32)
    att = sb("att", [128, 2, T], F32)
    htt = sb("htt", [128, 2, T], F32)
    hsc = sb("hsc", [128, T], F32)
    asc = sb("asc", [128, T], F32)
    zero1 = sb("zero1", [128, 1], F32)
    epsc = sb("epsc", [128, 1], F32)
    idb = sb("idb", [128, 128], BF16)
    carrs = sb("carrs", [128, 16], F32)

    pstp = [pst(f"pstp{i}", [128, 512], BF16) for i in range(2)]
    pszb = pst("pszb", [128, 2048], F32)

    @dma_stage(1)
    def s_loadx(eng, ds):
        eng.dma_start(out=xs[:], in_=x.rearrange("(t q) c -> q t c", q=128)).then_inc(ds, 16)
        return 1
    ldx = p.stage("sync", s_loadx, deps=[], dma=True)

    @dma_stage(4 + 16 + 1)
    def s_load(eng, ds):
        n = 0
        for i, nm in enumerate(("w_zf", "w_hf", "w_zs", "w_hs")):
            eng.dma_start(out=wall[:, :, i * 512:(i + 1) * 512],
                          in_=wz[nm].rearrange("(k q) c -> q k c", q=128)).then_inc(ds, 16)
            n += 1
        for j in range(4):
            for i, nm in enumerate(("b_zf", "b_hf", "b_zs", "b_hs")):
                eng.dma_start(out=ball[:, i * 4 + j:i * 4 + j + 1],
                              in_=bz[nm][0:1, j * 128:(j + 1) * 128].rearrange("o p -> p o")).then_inc(ds, 16)
                n += 1
        eng.dma_start(out=idb[:], in_=identb).then_inc(ds, 16)
        n += 1
        return n
    ld = p.stage("gpsimd", s_load, deps=[], dma=True)

    def s_stats(v):
        v.memset(zero1[:], 0.0)
        v.memset(epsc[:], EPS)
        v.tensor_scalar(nball[:], ball[:], -1.0, None, ALU.mult)
        ins = None
        for t in range(NT):
            v.tensor_reduce(stat[:, 0, t:t + 1], xs[:, t, :], AX.X, ALU.add)
            ins = v.scalar_tensor_tensor(sqs[:], xs[:, t, :], 1.0, xs[:, t, :],
                                         ALU.bypass, ALU.mult, accum_out=stat[:, 1, t:t + 1])
        p.chain("vector", v, ins)
        v.tensor_scalar(stat[:, 2, :], stat[:, 0, :], 1.0 / D, None, ALU.mult)
        ins = v.tensor_scalar(stat[:, 3, :], stat[:, 1, :], 1.0 / D, None, ALU.mult)
        p.chain("vector", v, ins)
        ins = v.scalar_tensor_tensor(stat[:, 0, :], stat[:, 2, :], 1.0, stat[:, 2, :],
                                     ALU.bypass, ALU.mult)
        p.chain("vector", v, ins)
        return v.tensor_sub(stat[:, 3, :], stat[:, 3, :], stat[:, 0, :])
    st_stats = p.stage("vector", s_stats, deps=[ld, ldx])

    def s_sqrt(sc):
        return sc.activation(stat[:, 4, :], stat[:, 3, :], AF.Sqrt, bias=epsc[:], scale=1.0)
    st_sqrt = p.stage("scalar", s_sqrt, deps=[st_stats])

    def s_recip(v):
        ins = v.reciprocal(stat[:, 3, :], stat[:, 4, :])
        p.chain("vector", v, ins)
        return v.scalar_tensor_tensor(stat[:, 0, :], stat[:, 2, :], -1.0,
                                      stat[:, 3, :], ALU.mult, ALU.mult)
    st_recip = p.stage("vector", s_recip, deps=[st_sqrt])

    def make_norm(t):
        def f(sc):
            return sc.activation(lnt[:, t % 2, :], xs[:, t, :], AF.Identity,
                                 bias=stat[:, 0, t:t + 1], scale=stat[:, 3, t:t + 1])
        return f

    def make_tp(t, c0, slot):
        def f(te):
            ins = None
            for i in range(4):
                ins = te.transpose(pstp[slot][:, i * 128:(i + 1) * 128],
                                   lnt[:, t % 2, (c0 + i) * 128:(c0 + i + 1) * 128], idb[:])
            return ins
        return f

    def make_tpe(t, c0, slot):
        def f(v):
            return v.tensor_copy(
                xT[:, c0:c0 + 4, t * 128:(t + 1) * 128],
                pstp[slot][:].rearrange("p (c q) -> p c q", c=4))
        return f

    prev_evac = {0: None, 1: None}
    prev_tp2 = {0: None, 1: None}
    all_tp = []
    gi = 0
    for t in range(NT):
        deps_n = [st_recip]
        if prev_tp2[t % 2] is not None:
            deps_n.append(prev_tp2[t % 2])
        sid_n = p.stage("scalar", make_norm(t), deps=deps_n)
        for c0 in (0, 4):
            slot = gi % 2
            deps = [sid_n]
            if prev_evac[slot] is not None:
                deps.append(prev_evac[slot])
            sid_t = p.stage("tensor", make_tp(t, c0, slot), deps=deps)
            sid_e = p.stage("vector", make_tpe(t, c0, slot), deps=[sid_t])
            prev_evac[slot] = sid_e
            prev_tp2[t % 2] = sid_t
            all_tp.append(sid_e)
            gi += 1

    def make_mmh(c, off, hf):
        j = c % 4

        def f(te):
            ins = None
            for n in (2 * hf, 2 * hf + 1):
                for k in range(DCH):
                    ins = te.matmul(pszb[:, n * 512:(n + 1) * 512],
                                    wall[:, k, off + j * 128: off + j * 128 + 128],
                                    xT[:, k, n * 512:(n + 1) * 512],
                                    start=(k == 0), stop=(k == DCH - 1))
            return ins
        return f

    def make_evz(c, hf):
        zb = (0 if c < 4 else 8) + (c % 4)
        cs = slice(hf * 1024, (hf + 1) * 1024)

        def f(sc):
            sc.activation(zt[:, c % 2, cs], pszb[:, cs], AF.Sigmoid,
                          bias=ball[:, zb:zb + 1], scale=1.0)
            return sc.activation(att[:, c % 2, cs], pszb[:, cs], AF.Sigmoid,
                                 bias=nball[:, zb:zb + 1], scale=-1.0)
        return f

    def make_evh(c, hf):
        hb = (4 if c < 4 else 12) + (c % 4)
        cs = slice(hf * 1024, (hf + 1) * 1024)

        def f(sc):
            return sc.activation(htt[:, c % 2, cs], pszb[:, cs], AF.Identity,
                                 bias=ball[:, hb:hb + 1], scale=1.0)
        return f

    def make_scan(c):
        def f(v):
            ins = v.tensor_tensor(asc[:], zt[:, c % 2, :], htt[:, c % 2, :], ALU.mult)
            p.chain("vector", v, ins)
            ins = v.tensor_tensor_scan(hsc[:], att[:, c % 2, :], asc[:], 0.0, ALU.mult, ALU.add)
            p.chain("vector", v, ins)
            ins = v.tensor_tensor_scan(asc[:], att[:, c % 2, :], zero1[:].broadcast_to([128, T]),
                                       1.0, ALU.mult, ALU.add)
            p.chain("vector", v, ins)
            v.tensor_copy(carrs[:, c:c + 1], asc[:, T - 1:T])
            return v.tensor_copy(carrs[:, 8 + c:9 + c], hsc[:, T - 1:T])
        return f

    def make_hout(c):
        @dma_stage(2)
        def f(eng, ds):
            eng.dma_start(out=h_out[c * 128:(c + 1) * 128, :], in_=hsc[:]).then_inc(ds, 16)
            eng.dma_start(out=a_out[c * 128:(c + 1) * 128, :], in_=asc[:]).then_inc(ds, 16)
            return 2
        return f

    hout_ids = []
    scans = {}
    prev_evh = {0: None, 1: None}
    for c in range(NSC):
        zoff = 0 if c < 4 else 1024
        hoff = 512 if c < 4 else 1536
        ez = {}
        eh = {}
        for hf in range(2):
            deps_m = list(all_tp)
            if prev_evh[hf] is not None:
                deps_m.append(prev_evh[hf])
            sid_m = p.stage("tensor", make_mmh(c, zoff, hf), deps=deps_m)
            deps_e = [sid_m]
            if c >= 2 and hf == 0:
                deps_e.append(scans[c - 2])
            ez[hf] = p.stage("scalar", make_evz(c, hf), deps=deps_e)
        for hf in range(2):
            sid_m = p.stage("tensor", make_mmh(c, hoff, hf), deps=[ez[hf]])
            deps_e = [sid_m]
            if c >= 2 and hf == 0:
                deps_e.append(scans[c - 2])
            eh[hf] = p.stage("scalar", make_evh(c, hf), deps=deps_e)
            prev_evh[hf] = eh[hf]
        deps_s = [ez[1], eh[1]]
        if c >= 1:
            deps_s.append(hout_ids[c - 1])
        sid_s = p.stage("vector", make_scan(c), deps=deps_s)
        scans[c] = sid_s
        sid_o = p.stage("gpsimd", make_hout(c), deps=[sid_s], dma=True)
        hout_ids.append(sid_o)

    @dma_stage(2 * NSC)
    def s_carr(eng, ds):
        for c in range(NSC):
            eng.dma_start(out=carr[0:1, c * 128:(c + 1) * 128].rearrange("o p -> p o"),
                          in_=carrs[:, c:c + 1]).then_inc(ds, 16)
            eng.dma_start(out=carr[1:2, c * 128:(c + 1) * 128].rearrange("o p -> p o"),
                          in_=carrs[:, 8 + c:9 + c]).then_inc(ds, 16)
        return 2 * NSC
    p.stage("gpsimd", s_carr, dma=True)
    p.stage("gpsimd", lambda g: g.engine_nop(), deps=None)

    p.build()
    return p


def build_phase2():
    p = Prog("p2")
    nc = p.nc
    x = nc.dram_tensor("x", [T, D], F32, kind="ExternalInput").ap()
    h_in = nc.dram_tensor("h_in", [D, T], F32, kind="ExternalInput").ap()
    a_in = nc.dram_tensor("a_in", [D, T], F32, kind="ExternalInput").ap()
    carrP = nc.dram_tensor("carrP", [8, D], F32, kind="ExternalInput").ap()
    carrH = nc.dram_tensor("carrH", [8, D], F32, kind="ExternalInput").ap()
    prevmask = nc.dram_tensor("prevmask", [128, 7], F32, kind="ExternalInput").ap()
    w_g = nc.dram_tensor("w_g", [D, D], F32, kind="ExternalInput").ap()
    b_g = nc.dram_tensor("b_g", [1, D], F32, kind="ExternalInput").ap()
    identb = nc.dram_tensor("identb", [128, 128], BF16, kind="ExternalInput").ap()
    identf = nc.dram_tensor("identf", [128, 128], F32, kind="ExternalInput").ap()
    x2o = nc.dram_tensor("x2", [T, D], F32, kind="ExternalOutput").ap()
    stats3 = nc.dram_tensor("stats3", [128, 2 * NT], F32, kind="ExternalOutput").ap()
    sxp = nc.dram_tensor("sxp", [1, 1], F32, kind="ExternalOutput").ap()

    es = ExitStack()
    p._es = es
    es.__enter__()
    sb = lambda name, shape, dt: es.enter_context(nc.sbuf_tensor(name, shape, dt))
    pst = lambda name, shape, dt: es.enter_context(nc.psum_tensor(name, shape, dt))

    hfm = sb("hfm", [128, NSC, T], F32)
    hbf = sb("hbf", [128, NSC, T], BF16)
    wg = sb("wg", [128, DCH, D], BF16)
    bgall = sb("bgall", [128, DCH], F32)
    cP = sb("cP", [8, D], F32)
    cH = sb("cH", [8, D], F32)
    msk = sb("msk", [128, 7], F32)
    omm = sb("omm", [128, 7], F32)
    PH = sb("PH", [128, NSC, 16], F32)
    Hc = sb("Hc", [128, NSC], F32)
    Qj = sb("Qj", [128, NSC], F32)
    Gj = sb("Gj", [128, NSC], F32)
    sio = sb("sio", [128, 4, T], F32)
    gsg = sb("gsg", [128, 2, T], F32)
    att_t = sb("att_t", [128, 2, D], F32)
    xtl = sb("xtl", [128, 2, D], F32)
    x2t = sb("x2t", [128, 2, D], F32)
    xl3 = sb("xl3", [128, 2, D], F32)
    sqs = sb("sqs", [128, D], F32)
    stc = sb("stc", [128, 2, 8], F32)
    m3s = sb("m3s", [128, NT], F32)
    r3s = sb("r3s", [128, NT], F32)
    axc = sb("axc", [128, 2], F32)
    axr = sb("axr", [128, 2], F32)
    axf = sb("axf", [128, 1], F32)
    epsc = sb("epsc", [128, 1], F32)
    idb = sb("idb", [128, 128], BF16)
    idf = sb("idf", [128, 128], F32)

    psG = [pst(f"psG{i}", [128, 1024], F32) for i in range(2)]
    psT = [pst(f"psT{i}", [128, 1024], F32) for i in range(2)]

    @dma_stage(4 + DCH + 1 + DCH)
    def s_load(eng, ds):
        n = 0
        eng.dma_start(out=cP[:], in_=carrP).then_inc(ds, 16); n += 1
        eng.dma_start(out=cH[:], in_=carrH).then_inc(ds, 16); n += 1
        eng.dma_start(out=msk[:], in_=prevmask).then_inc(ds, 16); n += 1
        eng.dma_start(out=idb[:], in_=identb).then_inc(ds, 16); n += 1
        for k in range(DCH):
            eng.dma_start(out=wg[:, k, :], in_=w_g[k * 128:(k + 1) * 128, :]).then_inc(ds, 16)
            n += 1
        eng.dma_start(out=idf[:], in_=identf).then_inc(ds, 16); n += 1
        for j in range(DCH):
            eng.dma_start(out=bgall[:, j:j + 1],
                          in_=b_g[0:1, j * 128:(j + 1) * 128].rearrange("o p -> p o")).then_inc(ds, 16)
            n += 1
        return n
    ld = p.stage("gpsimd", s_load, deps=[], dma=True)

    def s_prep(v):
        v.memset(epsc[:], EPS)
        v.memset(Hc[:], 0.0)
        v.memset(axc[:], 0.0)
        return v.tensor_scalar(omm[:], msk[:], -1.0, 1.0, ALU.mult, ALU.add)
    st_prep = p.stage("vector", s_prep, deps=[ld])

    def make_ctp(r):
        def f(te):
            ins = None
            for q in range(4):
                c = r * 4 + q
                te.transpose(psT[r % 2][:, q * 256:q * 256 + 8],
                             cP[:, c * 128:(c + 1) * 128], idf[0:8, 0:8])
                ins = te.transpose(psT[r % 2][:, q * 256 + 8:q * 256 + 16],
                                   cH[:, c * 128:(c + 1) * 128], idf[0:8, 0:8])
            return ins
        return f

    def make_ctpe(r):
        def f(v):
            ins = None
            for q in range(4):
                c = r * 4 + q
                ins = v.tensor_copy(PH[:, c, :], psT[r % 2][:, q * 256:q * 256 + 16])
            return ins
        return f

    ctp_done = None
    for r in range(2):
        sid_t = p.stage("tensor", make_ctp(r), deps=[ld])
        ctp_done = p.stage("vector", make_ctpe(r), deps=[sid_t])

    def s_fold(v):
        ins = None
        for j in range(7):
            v.tensor_scalar(Qj[:], PH[:, :, j:j + 1].squeeze(2), msk[:, j:j + 1],
                            omm[:, j:j + 1], ALU.mult, ALU.add)
            ins = v.tensor_scalar(Gj[:], PH[:, :, 8 + j:9 + j].squeeze(2), msk[:, j:j + 1],
                                  None, ALU.mult)
            p.chain("vector", v, ins)
            ins = v.tensor_tensor(Hc[:], Hc[:], Qj[:], ALU.mult)
            p.chain("vector", v, ins)
            ins = v.tensor_tensor(Hc[:], Hc[:], Gj[:], ALU.add)
            p.chain("vector", v, ins)
        return v.engine_nop()
    st_fold = p.stage("vector", s_fold, deps=[st_prep, ctp_done])

    def make_hload2(g):
        @dma_stage(2)
        def f(eng, ds):
            c0 = g * 2
            base = (g % 2) * 2
            eng.dma_start(out=sio[:, base:base + 2, :],
                          in_=h_in[c0 * 128:(c0 + 2) * 128, :].rearrange("(s q) c -> q s c", q=128)).then_inc(ds, 16)
            eng.dma_start(out=hfm[:, c0:c0 + 2, :],
                          in_=a_in[c0 * 128:(c0 + 2) * 128, :].rearrange("(s q) c -> q s c", q=128)).then_inc(ds, 16)
            return 2
        return f

    def make_fix(g):
        def f(v):
            ins = None
            for j in range(2):
                c = g * 2 + j
                base = (g % 2) * 2
                ins = v.scalar_tensor_tensor(hfm[:, c, :], hfm[:, c, :], Hc[:, c:c + 1],
                                             sio[:, base + j, :], ALU.mult, ALU.add)
            p.chain("vector", v, ins)
            for j in range(2):
                c = g * 2 + j
                ins = v.tensor_copy(hbf[:, c, :], hfm[:, c, :])
            return ins
        return f

    prev_fix = None
    fix_done = None
    for g in range(NSC // 2):
        deps_l = []
        if prev_fix is not None:
            deps_l.append(prev_fix)
        sid_l = p.stage("sync", make_hload2(g), deps=deps_l, dma=True)
        prev_fix = p.stage("vector", make_fix(g), deps=[sid_l, st_fold])
    fix_done = prev_fix

    def make_gmm(g, hh):
        def f(te):
            ins = None
            for n in (hh * 2, hh * 2 + 1):
                for k in range(DCH):
                    ins = te.matmul(psG[hh][:, (n % 2) * 512:(n % 2) * 512 + 512],
                                    wg[:, k, g * 128:(g + 1) * 128],
                                    hbf[:, k, n * 512:(n + 1) * 512],
                                    start=(k == 0), stop=(k == DCH - 1))
            return ins
        return f

    def make_gevac(g, hh):
        def f(sc):
            return sc.activation(gsg[:, g % 2, hh * 1024:(hh + 1) * 1024], psG[hh][:],
                                 AF.Sigmoid, bias=bgall[:, g:g + 1], scale=1.0)
        return f

    def make_gmul(g):
        def f(gp):
            return gp.tensor_tensor(hfm[:, g, :], gsg[:, g % 2, :], hfm[:, g, :], ALU.mult)
        return f

    prev_ev = {0: None, 1: None}
    prev_mul = {}
    for g in range(NSC):
        evs = []
        for hh in range(2):
            deps_m = [fix_done]
            if prev_ev[hh] is not None:
                deps_m.append(prev_ev[hh])
            sid_m = p.stage("tensor", make_gmm(g, hh), deps=deps_m)
            deps_e = [sid_m]
            if g >= 2 and (g - 2) in prev_mul:
                deps_e.append(prev_mul[g - 2])
            sid_e = p.stage("scalar", make_gevac(g, hh), deps=deps_e)
            prev_ev[hh] = sid_e
            evs.append(sid_e)
        sid_u = p.stage("gpsimd", make_gmul(g), deps=evs)
        prev_mul[g] = sid_u
    attn_done = prev_mul[NSC - 1]

    def make_xload(t):
        @dma_stage(1)
        def f(eng, ds):
            eng.dma_start(out=xtl[:, t % 2, :], in_=x[t * 128:(t + 1) * 128, :]).then_inc(ds, 16)
            return 1
        return f

    def make_attp(t):
        def f(te):
            ins = None
            for c in range(NSC):
                ins = te.matmul(psT[t % 2][:, c * 128:(c + 1) * 128],
                                hfm[:, c, t * 128:(t + 1) * 128], idf[:],
                                start=True, stop=True)
            return ins
        return f

    def make_attpe(t):
        def f(s):
            return s.activation(att_t[:, t % 2, :], psT[t % 2][:], AF.Identity,
                                bias=0.0, scale=1.0, accum_out=stc[:, t % 2, 0:1])
        return f

    def make_ln2a(t):
        def f(v):
            sl = t % 2
            ins = v.scalar_tensor_tensor(sqs[:], att_t[:, sl, :], 1.0, att_t[:, sl, :],
                                         ALU.bypass, ALU.mult, accum_out=stc[:, sl, 1:2])
            p.chain("vector", v, ins)
            v.tensor_scalar(stc[:, sl, 2:3], stc[:, sl, 0:1], 1.0 / D, None, ALU.mult)
            ins = v.tensor_scalar(stc[:, sl, 3:4], stc[:, sl, 1:2], 1.0 / D, None, ALU.mult)
            p.chain("vector", v, ins)
            ins = v.scalar_tensor_tensor(stc[:, sl, 0:1], stc[:, sl, 2:3], 1.0, stc[:, sl, 2:3],
                                         ALU.bypass, ALU.mult)
            p.chain("vector", v, ins)
            return v.tensor_sub(stc[:, sl, 3:4], stc[:, sl, 3:4], stc[:, sl, 0:1])
        return f

    def make_ln2b(t):
        def f(s):
            sl = t % 2
            return s.activation(stc[:, sl, 4:5], stc[:, sl, 3:4], AF.Sqrt,
                                bias=epsc[:], scale=1.0)
        return f

    def make_ln2c(t):
        def f(v):
            sl = t % 2
            ins = v.reciprocal(stc[:, sl, 5:6], stc[:, sl, 4:5])
            p.chain("vector", v, ins)
            return v.scalar_tensor_tensor(stc[:, sl, 6:7], stc[:, sl, 2:3], -1.0,
                                          stc[:, sl, 5:6], ALU.mult, ALU.mult)
        return f

    def make_ln2d(t):
        def f(s):
            sl = t % 2
            return s.activation(xl3[:, sl, :], att_t[:, sl, :], AF.Identity,
                                bias=stc[:, sl, 6:7], scale=stc[:, sl, 5:6])
        return f

    def make_resid(t):
        def f(v):
            sl = t % 2
            ins = v.scalar_tensor_tensor(x2t[:, sl, :], xtl[:, sl, :], 1.0, xl3[:, sl, :],
                                         ALU.bypass, ALU.add, accum_out=stc[:, sl, 0:1])
            p.chain("vector", v, ins)
            ins = v.scalar_tensor_tensor(sqs[:], x2t[:, sl, :], 1.0, x2t[:, sl, :],
                                         ALU.bypass, ALU.mult, accum_out=stc[:, sl, 1:2])
            p.chain("vector", v, ins)
            v.tensor_scalar(stc[:, sl, 2:3], stc[:, sl, 0:1], 1.0 / D, None, ALU.mult)
            ins = v.tensor_scalar(stc[:, sl, 3:4], stc[:, sl, 1:2], 1.0 / D, None, ALU.mult)
            p.chain("vector", v, ins)
            ins = v.scalar_tensor_tensor(stc[:, sl, 0:1], stc[:, sl, 2:3], 1.0, stc[:, sl, 2:3],
                                         ALU.bypass, ALU.mult)
            p.chain("vector", v, ins)
            return v.tensor_sub(stc[:, sl, 3:4], stc[:, sl, 3:4], stc[:, sl, 0:1])
        return f

    def make_ln3b(t):
        def f(s):
            sl = t % 2
            return s.activation(stc[:, sl, 4:5], stc[:, sl, 3:4], AF.Sqrt,
                                bias=epsc[:], scale=1.0)
        return f

    def make_ln3c(t):
        def f(v):
            sl = t % 2
            v.reciprocal(r3s[:, t:t + 1], stc[:, sl, 4:5])
            ins = v.tensor_copy(m3s[:, t:t + 1], stc[:, sl, 2:3])
            p.chain("vector", v, ins)
            return v.scalar_tensor_tensor(stc[:, sl, 6:7], stc[:, sl, 2:3], -1.0,
                                          r3s[:, t:t + 1], ALU.mult, ALU.mult)
        return f

    def make_ln3d(t):
        def f(s):
            sl = t % 2
            return s.activation(xl3[:, sl, :], x2t[:, sl, :], AF.Identity,
                                bias=stc[:, sl, 6:7], scale=r3s[:, t:t + 1])
        return f

    def make_sx(t):
        def f(v):
            sl = t % 2
            ins = v.tensor_reduce(axr[:, sl:sl + 1], xl3[:, sl, :], AX.X, ALU.max,
                                  apply_absolute_value=True)
            p.chain("vector", v, ins)
            return v.tensor_tensor(axc[:, sl:sl + 1], axc[:, sl:sl + 1],
                                   axr[:, sl:sl + 1], ALU.max)
        return f

    def make_x2out(t):
        @dma_stage(1)
        def f(eng, ds):
            eng.dma_start(out=x2o[t * 128:(t + 1) * 128, :], in_=x2t[:, t % 2, :]).then_inc(ds, 16)
            return 1
        return f

    prev_tile = {}

    def front2(t):
        deps_x = [attn_done] if t < 2 else [attn_done, prev_tile[(t - 2, "resid")],
                                            prev_tile[(t - 2, "out")]]
        sid_xl = p.stage("sync", make_xload(t), deps=deps_x, dma=True)
        deps_tp = [attn_done]
        if t >= 2:
            deps_tp.append(prev_tile[(t - 2, "tpe")])
        sid_tp = p.stage("tensor", make_attp(t), deps=deps_tp)
        deps_te = [sid_tp]
        if t >= 2:
            deps_te.append(prev_tile[(t - 2, "ln2d")])
            deps_te.append(prev_tile[(t - 2, "resid")])
        sid_te = p.stage("scalar", make_attpe(t), deps=deps_te)
        sid_a = p.stage("vector", make_ln2a(t), deps=[sid_te])
        sid_b = p.stage("scalar", make_ln2b(t), deps=[sid_a])
        prev_tile[(t, "xl")] = sid_xl
        prev_tile[(t, "tpe")] = sid_te
        prev_tile[(t, "ln2b")] = sid_b

    def back2(t):
        sid_c = p.stage("vector", make_ln2c(t), deps=[prev_tile[(t, "ln2b")]])
        deps_d = [sid_c]
        if t >= 2:
            deps_d.append(prev_tile[(t - 2, "resid")])
        sid_d = p.stage("scalar", make_ln2d(t), deps=deps_d)
        deps_r = [sid_d, prev_tile[(t, "xl")]]
        if t >= 2:
            deps_r.append(prev_tile[(t - 2, "out")])
        sid_r = p.stage("vector", make_resid(t), deps=deps_r)
        sid_b3 = p.stage("scalar", make_ln3b(t), deps=[sid_r])
        sid_c3 = p.stage("vector", make_ln3c(t), deps=[sid_b3])
        deps_d3 = [sid_c3]
        if t >= 2:
            deps_d3.append(prev_tile[(t - 2, "sx")])
        sid_d3 = p.stage("scalar", make_ln3d(t), deps=deps_d3)
        sid_sx = p.stage("vector", make_sx(t), deps=[sid_d3])
        sid_o = p.stage("gpsimd", make_x2out(t), deps=[sid_r], dma=True)
        prev_tile[(t, "resid")] = sid_r
        prev_tile[(t, "ln2d")] = sid_d
        prev_tile[(t, "sx")] = sid_sx
        prev_tile[(t, "out")] = sid_o

    for t in range(NT + 1):
        if t < NT:
            front2(t)
        if t >= 1:
            back2(t - 1)

    def s_sxv(v):
        v.tensor_reduce(axf[:], axc[:], AX.X, ALU.max)
        return v.memset(stc[:, 0, 7:8], 0.0)
    p.stage("vector", s_sxv, deps=None)

    def s_sxred(g):
        return g.tensor_reduce(stc[0:1, 0, 7:8], axf[:], AX.C, ALU.max)
    p.stage("gpsimd", s_sxred, deps=None)

    @dma_stage(3)
    def s_souts(eng, ds):
        eng.dma_start(out=stats3[:, 0:NT], in_=m3s[:]).then_inc(ds, 16)
        eng.dma_start(out=stats3[:, NT:2 * NT], in_=r3s[:]).then_inc(ds, 16)
        eng.dma_start(out=sxp, in_=stc[0:1, 0, 7:8]).then_inc(ds, 16)
        return 3
    p.stage("gpsimd", s_souts, deps=None, dma=True)
    p.stage("gpsimd", lambda g: g.engine_nop(), deps=None)

    p.build()
    return p


def build_phase3():
    p = Prog("p3")
    nc = p.nc
    x2 = nc.dram_tensor("x2", [T, D], F32, kind="ExternalInput").ap()
    stats3t = nc.dram_tensor("stats3t", [128, 2 * NT], F32, kind="ExternalInput").ap()
    sxall = nc.dram_tensor("sxall", [128, 8], F32, kind="ExternalInput").ap()
    w_up = nc.dram_tensor("w_up", [FFI, D], F32, kind="ExternalInput").ap()
    w_gate = nc.dram_tensor("w_gate", [FFI, D], F32, kind="ExternalInput").ap()
    identb = nc.dram_tensor("identb", [128, 128], BF16, kind="ExternalInput").ap()
    up_out = nc.dram_tensor("up_out", [FFI, T], F32, kind="ExternalOutput").ap()
    gt_out = nc.dram_tensor("gt_out", [FFI, T], F32, kind="ExternalOutput").ap()
    mxp = nc.dram_tensor("mxp", [1, 2], F32, kind="ExternalOutput").ap()

    es = ExitStack()
    p._es = es
    es.__enter__()
    sb = lambda name, shape, dt: es.enter_context(nc.sbuf_tensor(name, shape, dt))
    pst = lambda name, shape, dt: es.enter_context(nc.psum_tensor(name, shape, dt))

    xqT = sb("xqT", [128, DCH, T], BF16)
    wTu = sb("wTu", [128, DCH, FFI], BF16)
    wTg = sb("wTg", [128, DCH, FFI], BF16)
    s3 = sb("s3", [128, 2 * NT], F32)
    sxa = sb("sxa", [128, 8], F32)
    sxc = sb("sxc", [128, 4], F32)
    rsb = sb("rsb", [128, NT], F32)
    bsb = sb("bsb", [128, NT], F32)
    x2t = sb("x2t", [128, 4, D], F32)
    uq = sb("uq", [128, 2, D], F32)
    dqi = sb("dqi", [128, 2, D], I16)
    xqtile = sb("xqtile", [128, 2, D], BF16)
    wstr = sb("wstr", [128, 4, D], F32)
    bm = sb("bm", [128, 2, 32], F32)
    sbc = sb("sbc", [128, 2, 32], F32)
    sbi = sb("sbi", [128, 2, 32], F32)
    d8 = sb("d8", [128, 2, D], I8)
    wqb = sb("wqb", [128, 2, D], BF16)
    upc = sb("upc", [128, 2, T], F32)
    acc = sb("acc", [128, 2], F32)
    axr = sb("axr", [128, 2], F32)
    redo = sb("redo", [1, 2], F32)
    idb = sb("idb", [128, 128], BF16)

    psA = pst("psA", [128, 2048], F32)
    psB = pst("psB", [128, 2048], F32)
    PS = [psA, psB]

    @dma_stage(3)
    def s_load(eng, ds):
        eng.dma_start(out=s3[:], in_=stats3t).then_inc(ds, 16)
        eng.dma_start(out=sxa[:], in_=sxall).then_inc(ds, 16)
        eng.dma_start(out=idb[:], in_=identb).then_inc(ds, 16)
        return 3
    ld = p.stage("gpsimd", s_load, deps=[], dma=True)

    def s_sx(v):
        v.memset(acc[:], 0.0)
        ins = v.tensor_reduce(sxc[:, 0:1], sxa[:], AX.X, ALU.max)
        p.chain("vector", v, ins)
        ins = v.tensor_scalar(sxc[:, 1:2], sxc[:, 0:1], 1.0 / QA, 1e-12, ALU.mult, ALU.max)
        p.chain("vector", v, ins)
        ins = v.reciprocal(sxc[:, 2:3], sxc[:, 1:2])
        p.chain("vector", v, ins)
        ins = v.tensor_scalar(rsb[:], s3[:, NT:2 * NT], sxc[:, 2:3], None, ALU.mult)
        p.chain("vector", v, ins)
        return v.scalar_tensor_tensor(bsb[:], s3[:, 0:NT], -1.0, rsb[:],
                                      ALU.mult, ALU.mult)
    st_sx = p.stage("vector", s_sx, deps=[ld])

    def make_x2l(g):
        @dma_stage(1)
        def f(eng, ds):
            t0 = 2 * g
            sl0 = t0 % 4
            eng.dma_start(out=x2t[:, sl0:sl0 + 2, :],
                          in_=x2[t0 * 128:(t0 + 2) * 128, :].rearrange("(s q) c -> q s c", q=128)).then_inc(ds, 16)
            return 1
        return f

    def make_xqb(g):
        def f(s):
            ins = None
            for t in (2 * g, 2 * g + 1):
                ins = s.activation(uq[:, t % 2, :], x2t[:, t % 4, :], AF.Identity,
                                   bias=bsb[:, t:t + 1], scale=rsb[:, t:t + 1])
            return ins
        return f

    def make_xqc(g):
        def f(v):
            ins = None
            for t in (2 * g, 2 * g + 1):
                ins = v.tensor_scalar(dqi[:, t % 2, :], uq[:, t % 2, :], 1.0, None, ALU.mult)
            p.chain("vector", v, ins)
            for t in (2 * g, 2 * g + 1):
                ins = v.tensor_scalar(xqtile[:, t % 2, :], dqi[:, t % 2, :],
                                      sxc[:, 1:2], None, ALU.mult)
            return ins
        return f

    def make_xtp(g):
        def f(te):
            ins = None
            for t in (2 * g, 2 * g + 1):
                for c in range(DCH):
                    ins = te.matmul(PS[g % 2][:, (t % 2) * 1024 + c * 128:(t % 2) * 1024 + (c + 1) * 128],
                                    xqtile[:, t % 2, c * 128:(c + 1) * 128], idb[:],
                                    start=True, stop=True)
            return ins
        return f

    def make_xtpe(g):
        def f(s):
            ins = None
            for t in (2 * g, 2 * g + 1):
                ins = s.activation(
                    xqT[:, :, t * 128:(t + 1) * 128],
                    PS[g % 2][:, (t % 2) * 1024:(t % 2 + 1) * 1024].rearrange(
                        "p (c q) -> p c q", c=DCH),
                    AF.Identity, bias=0.0, scale=1.0)
            return ins
        return f

    prev_ps = {0: None, 1: None}
    prev_g = {}
    xq_done = []
    NG = NT // 2
    for g in range(NG):
        deps_l = [st_sx]
        if g >= 2:
            deps_l.append(prev_g[g - 2])
        sid_l = p.stage("sync", make_x2l(g), deps=deps_l, dma=True)
        sid_b = p.stage("scalar", make_xqb(g), deps=[sid_l, st_sx])
        sid_c = p.stage("vector", make_xqc(g), deps=[sid_b])
        deps_t = [sid_c]
        if prev_ps[g % 2] is not None:
            deps_t.append(prev_ps[g % 2])
        sid_t = p.stage("tensor", make_xtp(g), deps=deps_t)
        sid_e = p.stage("scalar", make_xtpe(g), deps=[sid_t])
        prev_ps[g % 2] = sid_e
        prev_g[g] = sid_e
        xq_done.append(sid_e)

    def make_wl(src_w, g):
        @dma_stage(1)
        def f(eng, ds):
            i0 = 2 * g
            sl0 = i0 % 4
            eng.dma_start(out=wstr[:, sl0:sl0 + 2, :],
                          in_=src_w[i0 * 128:(i0 + 2) * 128, :].rearrange("(s q) c -> q s c", q=128)).then_inc(ds, 16)
            return 1
        return f

    def make_wdqa(g):
        def f(v):
            ins = None
            for i in (2 * g, 2 * g + 1):
                ins = v.tensor_reduce(bm[:, i % 2, :], wstr[:, i % 4, :].rearrange("p (b q) -> p b q", q=32),
                                      AX.X, ALU.max, apply_absolute_value=True)
            p.chain("vector", v, ins)
            for i in (2 * g, 2 * g + 1):
                ins = v.tensor_scalar(sbc[:, i % 2, :], bm[:, i % 2, :], 1.0 / QW, 1e-12,
                                      ALU.mult, ALU.max)
            p.chain("vector", v, ins)
            for i in (2 * g, 2 * g + 1):
                ins = v.reciprocal(sbi[:, i % 2, :], sbc[:, i % 2, :])
            return ins
        return f

    def make_wdq2(g):
        def f(v):
            ins = None
            for i in (2 * g, 2 * g + 1):
                ins = v.tensor_tensor(d8[:, i % 2, :].rearrange("p (b q) -> p b q", q=32),
                                      wstr[:, i % 4, :].rearrange("p (b q) -> p b q", q=32),
                                      sbi[:, i % 2, :].unsqueeze(2).broadcast_to([128, 32, 32]),
                                      ALU.mult)
            p.chain("vector", v, ins)
            for i in (2 * g, 2 * g + 1):
                ins = v.tensor_tensor(wqb[:, i % 2, :].rearrange("p (b q) -> p b q", q=32),
                                      d8[:, i % 2, :].rearrange("p (b q) -> p b q", q=32),
                                      sbc[:, i % 2, :].unsqueeze(2).broadcast_to([128, 32, 32]),
                                      ALU.mult)
            return ins
        return f

    def make_wtp(g, slot):
        def f(te):
            ins = None
            for i in (2 * g, 2 * g + 1):
                for c in range(DCH):
                    ins = te.matmul(PS[slot][:, (i % 2) * 1024 + c * 128:(i % 2) * 1024 + (c + 1) * 128],
                                    wqb[:, i % 2, c * 128:(c + 1) * 128], idb[:],
                                    start=True, stop=True)
            return ins
        return f

    def make_wtpe(dst, g, slot):
        def f(s):
            ins = None
            for i in (2 * g, 2 * g + 1):
                ins = s.activation(
                    dst[:, :, i * 128:(i + 1) * 128],
                    PS[slot][:, (i % 2) * 1024:(i % 2 + 1) * 1024].rearrange(
                        "p (c q) -> p c q", c=DCH),
                    AF.Identity, bias=0.0, scale=1.0)
            return ins
        return f

    def make_mm(wT, i, ps):
        def f(te):
            ins = None
            for n in range(4):
                for k in range(DCH):
                    ins = te.matmul(ps[:, n * 512:(n + 1) * 512],
                                    wT[:, k, i * 128:(i + 1) * 128],
                                    xqT[:, k, n * 512:(n + 1) * 512],
                                    start=(k == 0), stop=(k == DCH - 1))
            return ins
        return f

    def make_mev(i, ps):
        def f(s):
            return s.activation(upc[:, i % 2, :], ps[:], AF.Identity, bias=0.0, scale=1.0)
        return f

    def make_max(i, a):
        def f(v):
            ins = v.tensor_reduce(axr[:, a:a + 1], upc[:, i % 2, :], AX.X, ALU.max,
                                  apply_absolute_value=True)
            p.chain("vector", v, ins)
            return v.tensor_tensor(acc[:, a:a + 1], acc[:, a:a + 1], axr[:, a:a + 1], ALU.max)
        return f

    def make_mout(dst, i0):
        @dma_stage(1)
        def f(eng, ds):
            eng.dma_start(out=dst[i0 * 128:(i0 + 2) * 128, :].rearrange("(s q) c -> q s c", q=128),
                          in_=upc[:, 0:2, :]).then_inc(ds, 16)
            return 1
        return f

    LAG = 1
    NWG = FCH // 2
    TWG = 2 * NWG
    fam = ((w_up, wTu, up_out), (w_gate, wTg, gt_out))
    rot = {"pu": NG}
    last_evac = dict(prev_ps)
    w_evac = {}
    prev_wt = {}
    prev_wb = {}
    prev_wd = {}
    mmstate = {"out": None, "max": {0: None, 1: None}}

    def emit_mm(m):
        wi, gl = divmod(m, NWG)
        wT, dst = fam[wi][1], fam[wi][2]
        evs = []
        for j in range(2):
            i = 2 * gl + j
            slot = rot["pu"] % 2
            rot["pu"] += 1
            deps_m = [xq_done[-1], w_evac[m]]
            if last_evac[slot] is not None:
                deps_m.append(last_evac[slot])
            sid_m = p.stage("tensor", make_mm(wT, i, PS[slot]), deps=deps_m)
            deps_e = [sid_m]
            if mmstate["out"] is not None:
                deps_e.append(mmstate["out"])
            if mmstate["max"][j] is not None:
                deps_e.append(mmstate["max"][j])
            sid_e = p.stage("scalar", make_mev(i, PS[slot]), deps=deps_e)
            last_evac[slot] = sid_e
            sid_x = p.stage("vector", make_max(i, wi), deps=[sid_e])
            mmstate["max"][j] = sid_x
            evs.append(sid_e)
        sid_o = p.stage("gpsimd", make_mout(dst, 2 * gl), deps=evs, dma=True)
        mmstate["out"] = sid_o

    dq_done = {}

    def emit_dequant(gg):
        wi, gl = divmod(gg, NWG)
        deps_l = []
        if gg >= 2:
            deps_l.append(prev_wb[gg - 2])
        sid_l = p.stage("sync" if gg % 2 == 0 else "gpsimd",
                        make_wl(fam[wi][0], gl), deps=deps_l, dma=True)
        sid_a = p.stage("vector", make_wdqa(gl), deps=[sid_l])
        deps_d = [sid_a]
        if gg >= 1:
            deps_d.append(prev_wt[gg - 1])
        sid_d = p.stage("vector", make_wdq2(gl), deps=deps_d)
        prev_wb[gg] = sid_d
        dq_done[gg] = sid_d

    emit_dequant(0)
    for gg in range(TWG + LAG):
        if gg < TWG:
            wi, gl = divmod(gg, NWG)
            dst_T = fam[wi][1]
            slot = rot["pu"] % 2
            rot["pu"] += 1
            deps_t = [dq_done[gg]]
            if last_evac[slot] is not None:
                deps_t.append(last_evac[slot])
            sid_t = p.stage("tensor", make_wtp(gl, slot), deps=deps_t)
            prev_wt[gg] = sid_t
            sid_e = p.stage("scalar", make_wtpe(dst_T, gl, slot), deps=[sid_t])
            last_evac[slot] = sid_e
            w_evac[gg] = sid_e
            if gg + 1 < TWG:
                emit_dequant(gg + 1)
        if gg >= LAG:
            emit_mm(gg - LAG)

    def s_fin(g):
        g.tensor_reduce(redo[0:1, 0:1], acc[:, 0:1], AX.C, ALU.max)
        return g.tensor_reduce(redo[0:1, 1:2], acc[:, 1:2], AX.C, ALU.max)
    p.stage("gpsimd", s_fin, deps=None)

    @dma_stage(1)
    def s_mxo(eng, ds):
        eng.dma_start(out=mxp, in_=redo[0:1, :]).then_inc(ds, 16)
        return 1
    p.stage("gpsimd", s_mxo, deps=None, dma=True)
    p.stage("gpsimd", lambda g: g.engine_nop(), deps=None)

    p.build()
    return p


def build_phase4():
    p = Prog("p4")
    nc = p.nc
    up_in = nc.dram_tensor("up_in", [FFI, T], F32, kind="ExternalInput").ap()
    gt_in = nc.dram_tensor("gt_in", [FFI, T], F32, kind="ExternalInput").ap()
    mxall = nc.dram_tensor("mxall", [128, 16], F32, kind="ExternalInput").ap()
    d_up = nc.dram_tensor("d_up", [FFI, T], I16, kind="ExternalOutput").ap()
    d_gt = nc.dram_tensor("d_gt", [FFI, T], I16, kind="ExternalOutput").ap()
    ghp = nc.dram_tensor("ghp", [1, 1], F32, kind="ExternalOutput").ap()
    scl = nc.dram_tensor("scl", [1, 2], F32, kind="ExternalOutput").ap()

    es = ExitStack()
    p._es = es
    es.__enter__()
    sb = lambda name, shape, dt: es.enter_context(nc.sbuf_tensor(name, shape, dt))

    G = 2
    ups = sb("ups", [128, 2 * G, T], F32)
    gts = sb("gts", [128, 2 * G, T], F32)
    dut = sb("dut", [128, 2 * G, T], I16)
    dgt = sb("dgt", [128, 2 * G, T], I16)
    sig = sb("sig", [128, 2 * G, T], F32)
    d2t = sb("d2t", [128, 2 * G, T], I32)
    ght = sb("ght", [128, 2 * G, T], F32)
    mxs = sb("mxs", [128, 16], F32)
    sc = sb("sc", [128, 6], F32)
    axr = sb("axr", [128, 2 * G], F32)
    acc = sb("acc", [128, 2 * G], F32)
    outs = sb("outs", [1, 3], F32)
    accf = sb("accf", [128, 1], F32)

    @dma_stage(1)
    def s_load(eng, ds):
        eng.dma_start(out=mxs[:], in_=mxall).then_inc(ds, 16)
        return 1
    ld = p.stage("gpsimd", s_load, deps=[], dma=True)

    def s_scales(v):
        v.memset(acc[:], 0.0)
        v.tensor_reduce(sc[:, 0:1], mxs[:, 0:8], AX.X, ALU.max)
        ins = v.tensor_reduce(sc[:, 3:4], mxs[:, 8:16], AX.X, ALU.max)
        p.chain("vector", v, ins)
        v.tensor_scalar(sc[:, 1:2], sc[:, 0:1], 1.0 / QA, 1e-12, ALU.mult, ALU.max)
        ins = v.tensor_scalar(sc[:, 4:5], sc[:, 3:4], 1.0 / QA, 1e-12, ALU.mult, ALU.max)
        p.chain("vector", v, ins)
        v.reciprocal(sc[:, 2:3], sc[:, 1:2])
        return v.reciprocal(sc[:, 5:6], sc[:, 4:5])
    st_sc = p.stage("vector", s_scales, deps=[ld])

    NL = 2 * G

    def make_l(i):
        @dma_stage(2)
        def f(eng, ds):
            sl = i % NL
            eng.dma_start(out=ups[:, sl, :],
                          in_=up_in[i * 128:(i + 1) * 128, :]).then_inc(ds, 16)
            eng.dma_start(out=gts[:, sl, :],
                          in_=gt_in[i * 128:(i + 1) * 128, :]).then_inc(ds, 16)
            return 2
        return f

    def make_q(i):
        def f(v):
            sl = i % NL
            v.tensor_scalar(dut[:, sl, :], ups[:, sl, :], sc[:, 2:3], None, ALU.mult)
            return v.tensor_scalar(dgt[:, sl, :], gts[:, sl, :], sc[:, 5:6], None, ALU.mult)
        return f

    def make_o(i):
        @dma_stage(2)
        def f(eng, ds):
            sl = i % NL
            eng.dma_start(out=d_up[i * 128:(i + 1) * 128, :],
                          in_=dut[:, sl, :]).then_inc(ds, 16)
            eng.dma_start(out=d_gt[i * 128:(i + 1) * 128, :],
                          in_=dgt[:, sl, :]).then_inc(ds, 16)
            return 2
        return f

    def make_sig(i):
        def f(s):
            sl = i % NL
            return s.activation(sig[:, sl, :], dgt[:, sl, :], AF.Sigmoid,
                                bias=0.0, scale=sc[:, 4:5])
        return f

    def make_d2(i):
        def f(v):
            sl = i % NL
            return v.tensor_scalar(d2t[:, sl, :], sig[:, sl, :], 65536.0, 65535.49,
                                   ALU.mult, ALU.min)
        return f

    def make_gh(i):
        def f(gp):
            sl = i % NL
            return gp.tensor_tensor(ght[:, sl, :], dgt[:, sl, :], d2t[:, sl, :], ALU.mult)
        return f

    def make_mx(i):
        def f(v):
            sl = i % NL
            ins = v.tensor_reduce(axr[:, sl:sl + 1], ght[:, sl, :], AX.X, ALU.max,
                                  apply_absolute_value=True)
            p.chain("vector", v, ins)
            return v.tensor_tensor(acc[:, sl:sl + 1], acc[:, sl:sl + 1],
                                   axr[:, sl:sl + 1], ALU.max)
        return f

    st = {}

    def emit4(step):
        i = step
        if i < FCH:
            deps_l = [st_sc]
            if i >= NL:
                deps_l.append(st[(i - NL, "q")])
            st[(i, "l")] = p.stage("sync", make_l(i), deps=deps_l, dma=True)
        i = step - 1
        if 0 <= i < FCH:
            deps_q = [st[(i, "l")], st_sc]
            if i >= NL:
                deps_q += [st[(i - NL, "o")], st[(i - NL, "gh")]]
            sid_q = p.stage("vector", make_q(i), deps=deps_q)
            st[(i, "q")] = sid_q
            st[(i, "o")] = p.stage("gpsimd", make_o(i), deps=[sid_q], dma=True)
            deps_s = [sid_q]
            if i >= NL:
                deps_s.append(st[(i - NL, "d2")])
            st[(i, "sig")] = p.stage("scalar", make_sig(i), deps=deps_s)
        i = step - 2
        if 0 <= i < FCH:
            deps_d = [st[(i, "sig")]]
            if i >= NL:
                deps_d.append(st[(i - NL, "gh")])
            sid_d = p.stage("vector", make_d2(i), deps=deps_d)
            st[(i, "d2")] = sid_d
            deps_g = [sid_d]
            if i >= NL:
                deps_g.append(st[(i - NL, "mx")])
            st[(i, "gh")] = p.stage("gpsimd", make_gh(i), deps=deps_g)
        i = step - 3
        if 0 <= i < FCH:
            st[(i, "mx")] = p.stage("vector", make_mx(i), deps=[st[(i, "gh")]])

    for step in range(FCH + 3):
        emit4(step)

    def s_tree(v):
        return v.tensor_reduce(accf[:], acc[:], AX.X, ALU.max)
    p.stage("vector", s_tree, deps=None)

    def s_fin(g):
        g.tensor_reduce(outs[0:1, 0:1], accf[:], AX.C, ALU.max)
        g.tensor_copy(outs[0:1, 1:2], sc[0:1, 1:2])
        return g.tensor_copy(outs[0:1, 2:3], sc[0:1, 4:5])
    p.stage("gpsimd", s_fin, deps=None)

    @dma_stage(2)
    def s_out(eng, ds):
        eng.dma_start(out=ghp, in_=outs[0:1, 0:1]).then_inc(ds, 16)
        eng.dma_start(out=scl, in_=outs[0:1, 1:3]).then_inc(ds, 16)
        return 2
    p.stage("gpsimd", s_out, deps=None, dma=True)
    p.stage("gpsimd", lambda g: g.engine_nop(), deps=None)

    p.build()
    return p


def build_phase5():
    p = Prog("p5")
    nc = p.nc
    d_up = nc.dram_tensor("d_up", [FFI, T], I16, kind="ExternalInput").ap()
    d_gt = nc.dram_tensor("d_gt", [FFI, T], I16, kind="ExternalInput").ap()
    sclin = nc.dram_tensor("sclin", [128, 10], F32, kind="ExternalInput").ap()
    d_gh = nc.dram_tensor("d_gh", [FFI, T], I16, kind="ExternalOutput").ap()
    op = nc.dram_tensor("op", [1, 1], F32, kind="ExternalOutput").ap()
    scl5 = nc.dram_tensor("scl5", [1, 1], F32, kind="ExternalOutput").ap()

    es = ExitStack()
    p._es = es
    es.__enter__()
    sb = lambda name, shape, dt: es.enter_context(nc.sbuf_tensor(name, shape, dt))

    G = 2
    dus = sb("dus", [128, 2 * G, T], I16)
    dgs = sb("dgs", [128, 2 * G, T], I16)
    dghs = sb("dghs", [128, 2 * G, T], I16)
    sig = sb("sig", [128, 2 * G, T], F32)
    d2t = sb("d2t", [128, 2 * G, T], I32)
    ght = sb("ght", [128, 2 * G, T], F32)
    ot = sb("ot", [128, 2 * G, T], F32)
    sc = sb("sc", [128, 10], F32)
    fac = sb("fac", [128, 4], F32)
    axr = sb("axr", [128, 2 * G], F32)
    acc = sb("acc", [128, 2 * G], F32)
    accf = sb("accf", [128, 1], F32)
    outs = sb("outs", [1, 2], F32)

    @dma_stage(1)
    def s_load(eng, ds):
        eng.dma_start(out=sc[:], in_=sclin).then_inc(ds, 16)
        return 1
    ld = p.stage("gpsimd", s_load, deps=[], dma=True)

    def s_scales(v):
        v.memset(acc[:], 0.0)
        v.tensor_reduce(fac[:, 2:3], sc[:, 2:10], AX.X, ALU.max)
        ins = v.tensor_scalar(fac[:, 3:4], sc[:, 1:2], 1.0 / 65536.0, None, ALU.mult)
        p.chain("vector", v, ins)
        ins = v.tensor_tensor(fac[:, 2:3], fac[:, 2:3], fac[:, 3:4], ALU.mult)
        p.chain("vector", v, ins)
        ins = v.tensor_scalar(fac[:, 0:1], fac[:, 2:3], 1.0 / QA, 1e-12, ALU.mult, ALU.max)
        p.chain("vector", v, ins)
        ins = v.reciprocal(fac[:, 1:2], fac[:, 0:1])
        p.chain("vector", v, ins)
        return v.tensor_tensor(fac[:, 1:2], fac[:, 1:2], fac[:, 3:4], ALU.mult)
    st_sc = p.stage("vector", s_scales, deps=[ld])

    NL = 2 * G

    def make_l(i):
        @dma_stage(2)
        def f(eng, ds):
            sl = i % NL
            eng.dma_start(out=dus[:, sl, :],
                          in_=d_up[i * 128:(i + 1) * 128, :]).then_inc(ds, 16)
            eng.dma_start(out=dgs[:, sl, :],
                          in_=d_gt[i * 128:(i + 1) * 128, :]).then_inc(ds, 16)
            return 2
        return f

    def make_sig(i):
        def f(s):
            sl = i % NL
            return s.activation(sig[:, sl, :], dgs[:, sl, :], AF.Sigmoid,
                                bias=0.0, scale=sc[:, 1:2])
        return f

    def make_d2(i):
        def f(v):
            sl = i % NL
            return v.tensor_scalar(d2t[:, sl, :], sig[:, sl, :], 65536.0, 65535.49,
                                   ALU.mult, ALU.min)
        return f

    def make_gh(i):
        def f(gp):
            sl = i % NL
            return gp.tensor_tensor(ght[:, sl, :], dgs[:, sl, :], d2t[:, sl, :], ALU.mult)
        return f

    def make_dgh(i):
        def f(v):
            sl = i % NL
            return v.tensor_scalar(dghs[:, sl, :], ght[:, sl, :], fac[:, 1:2], None, ALU.mult)
        return f

    def make_o(i):
        @dma_stage(1)
        def f(eng, ds):
            sl = i % NL
            eng.dma_start(out=d_gh[i * 128:(i + 1) * 128, :],
                          in_=dghs[:, sl, :]).then_inc(ds, 16)
            return 1
        return f

    def make_ot(i):
        def f(gp):
            sl = i % NL
            return gp.tensor_tensor(ot[:, sl, :], dghs[:, sl, :], dus[:, sl, :], ALU.mult)
        return f

    def make_mx(i):
        def f(v):
            sl = i % NL
            ins = v.tensor_reduce(axr[:, sl:sl + 1], ot[:, sl, :], AX.X, ALU.max,
                                  apply_absolute_value=True)
            p.chain("vector", v, ins)
            return v.tensor_tensor(acc[:, sl:sl + 1], acc[:, sl:sl + 1],
                                   axr[:, sl:sl + 1], ALU.max)
        return f

    st = {}

    def emit5(step):
        i = step
        if i < FCH:
            deps_l = [st_sc]
            if i >= NL:
                deps_l += [st[(i - NL, "sig")], st[(i - NL, "ot")]]
            st[(i, "l")] = p.stage("sync", make_l(i), deps=deps_l, dma=True)
        i = step - 1
        if 0 <= i < FCH:
            deps_s = [st[(i, "l")], st_sc]
            if i >= NL:
                deps_s.append(st[(i - NL, "d2")])
            sid_s = p.stage("scalar", make_sig(i), deps=deps_s)
            st[(i, "sig")] = sid_s
            deps_d = [sid_s]
            if i >= NL:
                deps_d.append(st[(i - NL, "gh")])
            st[(i, "d2")] = p.stage("vector", make_d2(i), deps=deps_d)
        i = step - 2
        if 0 <= i < FCH:
            deps_g = [st[(i, "d2")]]
            if i >= NL:
                deps_g.append(st[(i - NL, "dgh")])
            sid_g = p.stage("gpsimd", make_gh(i), deps=deps_g)
            st[(i, "gh")] = sid_g
            deps_h = [sid_g]
            if i >= NL:
                deps_h += [st[(i - NL, "o")], st[(i - NL, "ot")]]
            st[(i, "dgh")] = p.stage("vector", make_dgh(i), deps=deps_h)
        i = step - 3
        if 0 <= i < FCH:
            st[(i, "o")] = p.stage("gpsimd", make_o(i), deps=[st[(i, "dgh")]], dma=True)
            deps_ot = [st[(i, "dgh")]]
            if i >= NL:
                deps_ot.append(st[(i - NL, "mx")])
            sid_ot = p.stage("gpsimd", make_ot(i), deps=deps_ot)
            st[(i, "ot")] = sid_ot
            st[(i, "mx")] = p.stage("vector", make_mx(i), deps=[sid_ot])

    for step in range(FCH + 3):
        emit5(step)

    def s_tree(v):
        return v.tensor_reduce(accf[:], acc[:], AX.X, ALU.max)
    p.stage("vector", s_tree, deps=None)

    def s_fin(g):
        g.tensor_reduce(outs[0:1, 0:1], accf[:], AX.C, ALU.max)
        return g.tensor_copy(outs[0:1, 1:2], fac[0:1, 0:1])
    p.stage("gpsimd", s_fin, deps=None)

    @dma_stage(2)
    def s_out(eng, ds):
        eng.dma_start(out=op, in_=outs[0:1, 0:1]).then_inc(ds, 16)
        eng.dma_start(out=scl5, in_=outs[0:1, 1:2]).then_inc(ds, 16)
        return 2
    p.stage("gpsimd", s_out, deps=None, dma=True)
    p.stage("gpsimd", lambda g: g.engine_nop(), deps=None)

    p.build()
    return p


def build_phase6():
    p = Prog("p6")
    nc = p.nc
    d_gh = nc.dram_tensor("d_gh", [FFI, T], I16, kind="ExternalInput").ap()
    d_up = nc.dram_tensor("d_up", [FFI, T], I16, kind="ExternalInput").ap()
    x2 = nc.dram_tensor("x2", [T, D], F32, kind="ExternalInput").ap()
    w_down = nc.dram_tensor("w_down", [D, FFI], F32, kind="ExternalInput").ap()
    scl6 = nc.dram_tensor("scl6", [128, 10], F32, kind="ExternalInput").ap()
    identb = nc.dram_tensor("identb", [128, 128], BF16, kind="ExternalInput").ap()
    y = nc.dram_tensor("y", [T, D], F32, kind="ExternalOutput").ap()

    es = ExitStack()
    p._es = es
    es.__enter__()
    sb = lambda name, shape, dt: es.enter_context(nc.sbuf_tensor(name, shape, dt))
    pst = lambda name, shape, dt: es.enter_context(nc.psum_tensor(name, shape, dt))

    TH = T // 2
    oq = sb("oq", [128, FCH, TH], BF16)
    wdT = sb("wdT", [128, FCH, D], BF16)
    dghs = sb("dghs", [128, 4, TH], I16)
    dus = sb("dus", [128, 4, TH], I16)
    ot = sb("ot", [128, 4, TH], F32)
    d4t = sb("d4t", [128, 4, TH], I16)
    wstr = sb("wstr", [128, 2, FFI], F32)
    bm = sb("bm", [128, 96], F32)
    sbc = sb("sbc", [128, 96], F32)
    sbi = sb("sbi", [128, 96], F32)
    d8 = sb("d8", [128, FFI], I8)
    wqb = sb("wqb", [128, 2, FFI], BF16)
    sc = sb("sc", [128, 10], F32)
    fac = sb("fac", [128, 4], F32)
    ffo = sb("ffo", [128, 2, D], F32)
    l4s = sb("l4s", [128, 2, 8], F32)
    sqs = sb("sqs", [128, D], F32)
    xl4 = sb("xl4", [128, 2, D], F32)
    yt = sb("yt", [128, 2, D], F32)
    epsc = sb("epsc", [128, 1], F32)
    idb = sb("idb", [128, 128], BF16)

    psA = pst("psA", [128, 1024], F32)
    psB = pst("psB", [128, 1024], F32)
    PS = [psA, psB]

    @dma_stage(2)
    def s_load(eng, ds):
        eng.dma_start(out=sc[:], in_=scl6).then_inc(ds, 16)
        eng.dma_start(out=idb[:], in_=identb).then_inc(ds, 16)
        return 2
    ld = p.stage("gpsimd", s_load, deps=[], dma=True)

    def s_so(v):
        v.memset(epsc[:], EPS)
        v.tensor_reduce(fac[:, 3:4], sc[:, 2:10], AX.X, ALU.max)
        ins = v.tensor_tensor(fac[:, 2:3], sc[:, 0:1], sc[:, 1:2], ALU.mult)
        p.chain("vector", v, ins)
        ins = v.tensor_tensor(fac[:, 3:4], fac[:, 3:4], fac[:, 2:3], ALU.mult)
        p.chain("vector", v, ins)
        ins = v.tensor_scalar(fac[:, 0:1], fac[:, 3:4], 1.0 / QA, 1e-12, ALU.mult, ALU.max)
        p.chain("vector", v, ins)
        ins = v.reciprocal(fac[:, 1:2], fac[:, 0:1])
        p.chain("vector", v, ins)
        return v.tensor_tensor(fac[:, 1:2], fac[:, 1:2], fac[:, 2:3], ALU.mult)
    st_so = p.stage("vector", s_so, deps=[ld])

    def make_wl(j):
        @dma_stage(1)
        def f(eng, ds):
            eng.dma_start(out=wstr[:, j % 2, :], in_=w_down[j * 128:(j + 1) * 128, :]).then_inc(ds, 16)
            return 1
        return f

    def make_wdq(j):
        def f(v):
            sl = j % 2
            wsl = j % 2
            ins = v.tensor_reduce(bm[:], wstr[:, sl, :].rearrange("p (b q) -> p b q", q=32),
                                  AX.X, ALU.max, apply_absolute_value=True)
            p.chain("vector", v, ins)
            ins = v.tensor_scalar(sbc[:], bm[:], 1.0 / QW, 1e-12, ALU.mult, ALU.max)
            p.chain("vector", v, ins)
            ins = v.reciprocal(sbi[:], sbc[:])
            p.chain("vector", v, ins)
            ins = v.tensor_tensor(d8[:].rearrange("p (b q) -> p b q", q=32),
                                  wstr[:, sl, :].rearrange("p (b q) -> p b q", q=32),
                                  sbi[:].unsqueeze(2).broadcast_to([128, 96, 32]), ALU.mult)
            p.chain("vector", v, ins)
            return v.tensor_tensor(wqb[:, wsl, :].rearrange("p (b q) -> p b q", q=32),
                                   d8[:].rearrange("p (b q) -> p b q", q=32),
                                   sbc[:].unsqueeze(2).broadcast_to([128, 96, 32]), ALU.mult)
        return f

    def make_wtp(j, r):
        def f(te):
            ins = None
            for q in range(8):
                c = r * 8 + q
                ins = te.matmul(PS[r % 2][:, q * 128:(q + 1) * 128],
                                wqb[:, j % 2, c * 128:(c + 1) * 128], idb[:],
                                start=True, stop=True)
            return ins
        return f

    def make_wtpe(j, r):
        def f(s):
            return s.activation(
                wdT[:, r * 8:(r + 1) * 8, j * 128:(j + 1) * 128],
                PS[r % 2][:].rearrange("p (c q) -> p c q", c=8),
                AF.Identity, bias=0.0, scale=1.0)
        return f

    prev_ps = {0: None, 1: None}
    prev_dq = {}
    w_last = None
    prev_wtp = {}
    for j in range(DCH):
        deps_l = [] if j < 2 else [prev_dq[j - 2]]
        sid_l = p.stage("sync", make_wl(j), deps=deps_l, dma=True)
        deps_d = [sid_l]
        if j >= 2:
            deps_d.append(prev_wtp[j - 2])
        sid_d = p.stage("vector", make_wdq(j), deps=deps_d)
        last_e = None
        last_t = None
        for r in range(3):
            deps = [sid_d]
            if prev_ps[r % 2] is not None:
                deps.append(prev_ps[r % 2])
            sid_t = p.stage("tensor", make_wtp(j, r), deps=deps)
            sid_e = p.stage("scalar", make_wtpe(j, r), deps=[sid_t])
            prev_ps[r % 2] = sid_e
            last_e = sid_e
            last_t = sid_t
        prev_dq[j] = sid_d
        prev_wtp[j] = last_t
        w_last = last_e

    def make_ol(h, i):
        @dma_stage(2)
        def f(eng, ds):
            cs = slice(h * TH, (h + 1) * TH)
            sl = (h * FCH + i) % 4
            eng.dma_start(out=dghs[:, sl, :], in_=d_gh[i * 128:(i + 1) * 128, cs]).then_inc(ds, 16)
            eng.dma_start(out=dus[:, sl, :], in_=d_up[i * 128:(i + 1) * 128, cs]).then_inc(ds, 16)
            return 2
        return f

    def make_oprod(h, i):
        def f(gp):
            sl = (h * FCH + i) % 4
            return gp.tensor_tensor(ot[:, sl, :], dghs[:, sl, :], dus[:, sl, :], ALU.mult)
        return f

    def make_oqa(h, i):
        def f(v):
            sl = (h * FCH + i) % 4
            return v.tensor_scalar(d4t[:, sl, :], ot[:, sl, :], fac[:, 1:2], None, ALU.mult)
        return f

    def make_oqb(h, i):
        def f(v):
            sl = (h * FCH + i) % 4
            return v.tensor_scalar(oq[:, i, :], d4t[:, sl, :], fac[:, 0:1], None, ALU.mult)
        return f

    ost = {}

    def emit_ol(ii):
        h, i = divmod(ii, FCH)
        deps = []
        if ii >= 4:
            deps.append(ost[(ii - 4, "op")])
        ost[(ii, "l")] = p.stage("gpsimd", make_ol(h, i), deps=deps, dma=True)

    def emit_oprod(ii):
        h, i = divmod(ii, FCH)
        deps = [ost[(ii, "l")]]
        if ii >= 4:
            deps.append(ost[(ii - 4, "qa")])
        ost[(ii, "op")] = p.stage("gpsimd", make_oprod(h, i), deps=deps)

    def emit_oqa(ii):
        h, i = divmod(ii, FCH)
        deps = [ost[(ii, "op")], st_so]
        if ii >= 4:
            deps.append(ost[(ii - 4, "qb")])
        ost[(ii, "qa")] = p.stage("vector", make_oqa(h, i), deps=deps)

    def emit_oqb(ii, barrier):
        h, i = divmod(ii, FCH)
        deps = [ost[(ii, "qa")], st_so]
        if barrier is not None:
            deps.append(barrier)
        ost[(ii, "qb")] = p.stage("vector", make_oqb(h, i), deps=deps)

    def make_x2l(t):
        @dma_stage(1)
        def f(eng, ds):
            eng.dma_start(out=yt[:, t % 2, :], in_=x2[t * 128:(t + 1) * 128, :]).then_inc(ds, 16)
            return 1
        return f

    def make_dmm(t, ps):
        def f(te):
            ins = None
            toff = (t % (NT // 2)) * 128
            for h2 in range(2):
                for k in range(FCH):
                    ins = te.matmul(ps[:, h2 * 512:(h2 + 1) * 512],
                                    oq[:, k, toff:toff + 128],
                                    wdT[:, k, h2 * 512:(h2 + 1) * 512],
                                    start=(k == 0), stop=(k == FCH - 1))
            return ins
        return f

    def make_ev(t, ps):
        def f(s):
            return s.activation(ffo[:, t % 2, :], ps[:], AF.Identity, bias=0.0,
                                scale=1.0, accum_out=l4s[:, t % 2, 0:1])
        return f

    def make_l4a(t):
        def f(v):
            sl = t % 2
            ins = v.scalar_tensor_tensor(sqs[:], ffo[:, sl, :], 1.0, ffo[:, sl, :],
                                         ALU.bypass, ALU.mult, accum_out=l4s[:, sl, 1:2])
            p.chain("vector", v, ins)
            v.tensor_scalar(l4s[:, sl, 2:3], l4s[:, sl, 0:1], 1.0 / D, None, ALU.mult)
            ins = v.tensor_scalar(l4s[:, sl, 3:4], l4s[:, sl, 1:2], 1.0 / D, None, ALU.mult)
            p.chain("vector", v, ins)
            ins = v.scalar_tensor_tensor(l4s[:, sl, 0:1], l4s[:, sl, 2:3], 1.0, l4s[:, sl, 2:3],
                                         ALU.bypass, ALU.mult)
            p.chain("vector", v, ins)
            return v.tensor_sub(l4s[:, sl, 3:4], l4s[:, sl, 3:4], l4s[:, sl, 0:1])
        return f

    def make_l4b(t):
        def f(s):
            return s.activation(l4s[:, t % 2, 4:5], l4s[:, t % 2, 3:4], AF.Sqrt,
                                bias=epsc[:], scale=1.0)
        return f

    def make_l4c(t):
        def f(v):
            sl = t % 2
            ins = v.reciprocal(l4s[:, sl, 5:6], l4s[:, sl, 4:5])
            p.chain("vector", v, ins)
            return v.scalar_tensor_tensor(l4s[:, sl, 6:7], l4s[:, sl, 2:3], -1.0,
                                          l4s[:, sl, 5:6], ALU.mult, ALU.mult)
        return f

    def make_l4d(t):
        def f(s):
            sl = t % 2
            return s.activation(xl4[:, sl, :], ffo[:, sl, :], AF.Identity,
                                bias=l4s[:, sl, 6:7], scale=l4s[:, sl, 5:6])
        return f

    def make_y(t):
        def f(v):
            sl = t % 2
            return v.tensor_tensor(yt[:, sl, :], yt[:, sl, :], xl4[:, sl, :], ALU.add)
        return f

    def make_yo(t):
        @dma_stage(1)
        def f(eng, ds):
            eng.dma_start(out=y[t * 128:(t + 1) * 128, :], in_=yt[:, t % 2, :]).then_inc(ds, 16)
            return 1
        return f

    prev_t = {}

    def front6(t, oq_last):
        deps_x = [] if t < 2 else [prev_t[(t - 2, "y")]]
        sid_xl = p.stage("sync", make_x2l(t), deps=deps_x, dma=True)
        ps = PS[t % 2]
        deps_m = [oq_last, w_last]
        if t >= 2:
            deps_m.append(prev_t[(t - 2, "ev")])
        sid_m = p.stage("tensor", make_dmm(t, ps), deps=deps_m)
        deps_e = [sid_m]
        if t >= 2:
            deps_e.append(prev_t[(t - 2, "l4d")])
        sid_e = p.stage("scalar", make_ev(t, ps), deps=deps_e)
        sid_a = p.stage("vector", make_l4a(t), deps=[sid_e])
        sid_b = p.stage("scalar", make_l4b(t), deps=[sid_a])
        prev_t[(t, "ev")] = sid_e
        prev_t[(t, "mm")] = sid_m
        prev_t[(t, "l4b")] = sid_b
        prev_t[(t, "xl")] = sid_xl

    def back6(t):
        sid_c = p.stage("vector", make_l4c(t), deps=[prev_t[(t, "l4b")]])
        sid_d = p.stage("scalar", make_l4d(t), deps=[sid_c])
        sid_y = p.stage("vector", make_y(t), deps=[sid_d, prev_t[(t, "xl")]])
        sid_o = p.stage("gpsimd", make_yo(t), deps=[sid_y], dma=True)
        prev_t[(t, "y")] = sid_o
        prev_t[(t, "l4d")] = sid_d

    for s in range(FCH + 2):
        if s < FCH:
            emit_ol(s)
        if 1 <= s <= FCH:
            emit_oprod(s - 1)
            emit_oqa(s - 1)
        if 2 <= s <= FCH + 1:
            emit_oqb(s - 2, None)
    oq_last0 = ost[(FCH - 1, "qb")]
    for ii in range(FCH, FCH + 4):
        emit_ol(ii)
    for ii in range(FCH, FCH + 2):
        emit_oprod(ii)
        emit_oqa(ii)
    for tt in range(NT // 2 + 1):
        if tt < NT // 2:
            front6(tt, oq_last0)
        if tt >= 1:
            back6(tt - 1)
    barrier = prev_t[(7, "mm")]
    for s in range(FCH):
        if FCH + 4 + s < 2 * FCH:
            emit_ol(FCH + 4 + s)
        if FCH + 2 + s < 2 * FCH:
            emit_oprod(FCH + 2 + s)
            emit_oqa(FCH + 2 + s)
        emit_oqb(FCH + s, barrier)
    oq_last1 = ost[(2 * FCH - 1, "qb")]
    for tt in range(NT // 2 + 1):
        t = NT // 2 + tt
        if tt < NT // 2:
            front6(t, oq_last1)
        if tt >= 1:
            back6(t - 1)
    p.stage("gpsimd", lambda g: g.engine_nop(), deps=None)

    p.build()
    return p



_PHASES = {}


def _get_phase(name, builder):
    if name not in _PHASES:
        _PHASES[name] = builder()
    return _PHASES[name]


def _shard_x(x):
    out = []
    for c in range(NCORES):
        b, m = divmod(c, 4)
        out.append(np.ascontiguousarray(x[b, m * T:(m + 1) * T, :]))
    return out


def kernel(**inputs):
    x = np.asarray(inputs["L0_attn_x"], dtype=np.float32)
    ident_b = np.eye(128, dtype=np.float32).astype(BF16NP)
    ident_f = np.eye(128, dtype=np.float32)
    f32 = lambda k: np.asarray(inputs[k], np.float32)

    shards = _shard_x(x)
    cores = list(range(NCORES))

    p1 = _get_phase("p1", build_phase1)
    im1 = [{
        "x": shards[c],
        "w_zf": f32("L0_attn_W_zf"), "w_hf": f32("L0_attn_W_hf"),
        "w_zs": f32("L0_attn_W_zs"), "w_hs": f32("L0_attn_W_hs"),
        "b_zf": f32("L0_attn_b_zf").reshape(1, DF),
        "b_hf": f32("L0_attn_b_hf").reshape(1, DF),
        "b_zs": f32("L0_attn_b_zs").reshape(1, DF),
        "b_hs": f32("L0_attn_b_hs").reshape(1, DF),
        "identb": ident_b,
    } for c in cores]
    r1 = run_bass_kernel_spmd(p1.nc, im1, core_ids=cores).results

    carrP = np.stack([np.asarray(r1[c]["carr"])[0] for c in cores])
    carrH = np.stack([np.asarray(r1[c]["carr"])[1] for c in cores])
    masks = []
    for c in cores:
        m = np.zeros((128, 7), np.float32)
        for j in range(7):
            if j // 4 == c // 4 and j < c:
                m[:, j] = 1.0
        masks.append(m)

    p2 = _get_phase("p2", build_phase2)
    im2 = [{
        "x": shards[c],
        "h_in": np.asarray(r1[c]["h_out"]),
        "a_in": np.asarray(r1[c]["a_out"]),
        "carrP": carrP, "carrH": carrH,
        "prevmask": masks[c],
        "w_g": f32("L0_attn_W_g"),
        "b_g": f32("L0_attn_b_g").reshape(1, D),
        "identb": ident_b, "identf": ident_f,
    } for c in cores]
    r2 = run_bass_kernel_spmd(p2.nc, im2, core_ids=cores).results

    sxall = np.zeros((128, 8), np.float32)
    for c in cores:
        sxall[:, c] = np.asarray(r2[c]["sxp"])[0, 0]

    p3 = _get_phase("p3", build_phase3)
    im3 = []
    for c in cores:
        im3.append({
            "x2": np.asarray(r2[c]["x2"]),
            "stats3t": np.asarray(r2[c]["stats3"]),
            "sxall": sxall,
            "w_up": f32("L0_ffn_w_up"), "w_gate": f32("L0_ffn_w_gate"),
            "identb": ident_b,
        })
    r3 = run_bass_kernel_spmd(p3.nc, im3, core_ids=cores).results

    mxall = np.zeros((128, 16), np.float32)
    for c in cores:
        mxall[:, c] = np.asarray(r3[c]["mxp"])[0, 0]
        mxall[:, 8 + c] = np.asarray(r3[c]["mxp"])[0, 1]

    p4 = _get_phase("p4", build_phase4)
    im4 = [{
        "up_in": np.asarray(r3[c]["up_out"]),
        "gt_in": np.asarray(r3[c]["gt_out"]),
        "mxall": mxall,
    } for c in cores]
    r4 = run_bass_kernel_spmd(p4.nc, im4, core_ids=cores).results

    sclin5 = np.zeros((128, 10), np.float32)
    sclin5[:, 0] = np.asarray(r4[0]["scl"])[0, 0]
    sclin5[:, 1] = np.asarray(r4[0]["scl"])[0, 1]
    for c in cores:
        sclin5[:, 2 + c] = np.asarray(r4[c]["ghp"])[0, 0]

    p5 = _get_phase("p5", build_phase5)
    im5 = [{
        "d_up": np.asarray(r4[c]["d_up"]),
        "d_gt": np.asarray(r4[c]["d_gt"]),
        "sclin": sclin5,
    } for c in cores]
    r5 = run_bass_kernel_spmd(p5.nc, im5, core_ids=cores).results

    scl6 = np.zeros((128, 10), np.float32)
    scl6[:, 0] = np.asarray(r4[0]["scl"])[0, 0]
    scl6[:, 1] = np.asarray(r5[0]["scl5"])[0, 0]
    for c in cores:
        scl6[:, 2 + c] = np.asarray(r5[c]["op"])[0, 0]

    p6 = _get_phase("p6", build_phase6)
    im6 = [{
        "d_gh": np.asarray(r5[c]["d_gh"]),
        "d_up": np.asarray(r4[c]["d_up"]),
        "x2": np.asarray(r2[c]["x2"]),
        "w_down": f32("L0_ffn_w_down"),
        "scl6": scl6,
        "identb": ident_b,
    } for c in cores]
    r6 = run_bass_kernel_spmd(p6.nc, im6, core_ids=cores).results

    out = np.empty((B, S, D), np.float32)
    for c in cores:
        b_, m_ = divmod(c, 4)
        out[b_, m_ * T:(m_ + 1) * T, :] = np.asarray(r6[c]["y"])
    return out


# version 15
# speedup vs baseline: 1.3220x; 1.0016x over previous
from contextlib import ExitStack

import numpy as np
import ml_dtypes

import concourse.bass as bass
import concourse.mybir as mybir
from concourse.bass_utils import run_bass_kernel_spmd

F32 = mybir.dt.float32
BF16 = mybir.dt.bfloat16
I16 = mybir.dt.int16
I32 = mybir.dt.int32
I8 = mybir.dt.int8
AF = mybir.ActivationFunctionType
ALU = mybir.AluOpType
AX = mybir.AxisListType

NCORES = 8
B, S, D = 2, 8192, 1024
T = 2048
NT = T // 128
DCH = D // 128
DF = 512
NSC = 8
FFI = 3072
FCH = FFI // 128
EPS = 1e-5
QA = 32767.0
QW = 7.0
BF16NP = ml_dtypes.bfloat16


class Prog:

    ENGS = ("sync", "gpsimd", "vector", "scalar", "tensor")

    def __init__(self, name):
        self.nc = bass.Bass("TRN2", target_bir_lowering=False)
        self.name = name
        self.stages = []
        self._icnt = {e: 0 for e in self.ENGS}

    def chain(self, engine, eng, ins):
        ins.then_inc(self.intra[engine], 1)
        self._icnt[engine] += 1
        eng.wait_ge(self.intra[engine], self._icnt[engine])

    def stage(self, engine, fn, deps=None, dma=False):
        assert engine in self.ENGS
        if dma:
            assert engine in ("sync", "gpsimd")
        self.stages.append((engine, fn, dma, deps))
        return len(self.stages) - 1

    def build(self):
        nc = self.nc
        es = self._es
        NDS = 4
        prog = {e: es.enter_context(nc.semaphore(f"pg_{e}")) for e in self.ENGS}
        self.intra = {e: es.enter_context(nc.semaphore(f"in_{e}")) for e in self.ENGS}
        dsem = {e: [es.enter_context(nc.semaphore(f"dm_{e}{i}")) for i in range(NDS)]
                for e in ("sync", "gpsimd")}

        marker = []
        n_stage = {e: 0 for e in self.ENGS}
        n_dma_stage = {"sync": 0, "gpsimd": 0}
        n_dma = {("sync", i): 0 for i in range(NDS)}
        n_dma.update({("gpsimd", i): 0 for i in range(NDS)})
        slot_of = {}
        for sid, (engine, fn, dma, deps) in enumerate(self.stages):
            if dma:
                slot = n_dma_stage[engine] % NDS
                slot_of[sid] = slot
                n_dma_stage[engine] += 1
                nd = getattr(fn, "n_dma")
                n_dma[(engine, slot)] += nd
                marker.append(("dma", engine, slot, n_dma[(engine, slot)]))
            else:
                n_stage[engine] += 1
                marker.append(("prog", engine, None, n_stage[engine]))

        per_engine = {e: [] for e in self.ENGS}
        for sid, (engine, fn, dma, deps) in enumerate(self.stages):
            dep_ids = list(range(sid)) if deps is None else list(deps)
            waits = {}
            for d in dep_ids:
                kind, de, slot, cnt = marker[d]
                if de == engine and kind == "prog" and not dma:
                    continue
                if de == engine and kind == "dma" and dma:
                    continue
                key = (kind, de, slot)
                waits[key] = max(waits.get(key, 0), cnt)
            per_engine[engine].append((sid, fn, dma, waits))

        block = es.enter_context(nc.Block())

        def make_body(engine, items):
            def body(eng):
                for sid, fn, dma, waits in items:
                    if not dma and engine not in ("sync",):
                        pcnt = marker[sid][3] - 1
                        if pcnt >= 1:
                            eng.wait_ge(prog[engine], pcnt)
                    for (kind, de, slot), cnt in sorted(
                            waits.items(), key=lambda kv: (kv[0][0], kv[0][1], kv[0][2] or 0)):
                        if kind == "prog":
                            eng.wait_ge(prog[de], cnt)
                        else:
                            eng.wait_ge(dsem[de][slot], 16 * cnt)
                    if dma:
                        n = fn(eng, dsem[engine][slot_of[sid]])
                        assert n == fn.n_dma, (self.name, sid, n, fn.n_dma)
                    else:
                        last = fn(eng)
                        assert last is not None, (self.name, sid, "stage must return last ins")
                        last.then_inc(prog[engine], 1)
            return body

        for e in self.ENGS:
            if per_engine[e]:
                getattr(block, e)(make_body(e, per_engine[e]))
        self.prog_sems = prog
        self.dma_sems = dsem


def dma_stage(n):
    def deco(fn):
        fn.n_dma = n
        return fn
    return deco



def build_phase1():
    p = Prog("p1")
    nc = p.nc
    x = nc.dram_tensor("x", [T, D], F32, kind="ExternalInput").ap()
    wz = {}
    for nm in ("w_zf", "w_hf", "w_zs", "w_hs"):
        wz[nm] = nc.dram_tensor(nm, [D, DF], F32, kind="ExternalInput").ap()
    bz = {}
    for nm in ("b_zf", "b_hf", "b_zs", "b_hs"):
        bz[nm] = nc.dram_tensor(nm, [1, DF], F32, kind="ExternalInput").ap()
    identb = nc.dram_tensor("identb", [128, 128], BF16, kind="ExternalInput").ap()
    h_out = nc.dram_tensor("h_out", [D, T], F32, kind="ExternalOutput").ap()
    a_out = nc.dram_tensor("a_out", [D, T], F32, kind="ExternalOutput").ap()
    carr = nc.dram_tensor("carr", [2, D], F32, kind="ExternalOutput").ap()

    es = ExitStack()
    p._es = es
    es.__enter__()
    sb = lambda name, shape, dt: es.enter_context(nc.sbuf_tensor(name, shape, dt))
    pst = lambda name, shape, dt: es.enter_context(nc.psum_tensor(name, shape, dt))

    xs = sb("xs", [128, NT, D], F32)
    stat = sb("stat", [128, 5, NT], F32)
    sqs = sb("sqs", [128, D], F32)
    lnt = sb("lnt", [128, 2, D], BF16)
    xT = sb("xT", [128, DCH, T], BF16)
    wall = sb("wall", [128, DCH, 2048], BF16)
    ball = sb("ball", [128, 16], F32)
    nball = sb("nball", [128, 16], F32)
    zt = sb("zt", [128, 2, T], F32)
    att = sb("att", [128, 2, T], F32)
    htt = sb("htt", [128, 2, T], F32)
    hsc = sb("hsc", [128, T], F32)
    asc = sb("asc", [128, T], F32)
    zero1 = sb("zero1", [128, 1], F32)
    epsc = sb("epsc", [128, 1], F32)
    idb = sb("idb", [128, 128], BF16)
    carrs = sb("carrs", [128, 16], F32)

    pstp = [pst(f"pstp{i}", [128, 512], BF16) for i in range(2)]
    pszb = pst("pszb", [128, 2048], F32)

    @dma_stage(1)
    def s_loadx(eng, ds):
        eng.dma_start(out=xs[:], in_=x.rearrange("(t q) c -> q t c", q=128)).then_inc(ds, 16)
        return 1
    ldx = p.stage("sync", s_loadx, deps=[], dma=True)

    @dma_stage(4 + 16 + 1)
    def s_load(eng, ds):
        n = 0
        for i, nm in enumerate(("w_zf", "w_hf", "w_zs", "w_hs")):
            eng.dma_start(out=wall[:, :, i * 512:(i + 1) * 512],
                          in_=wz[nm].rearrange("(k q) c -> q k c", q=128)).then_inc(ds, 16)
            n += 1
        for j in range(4):
            for i, nm in enumerate(("b_zf", "b_hf", "b_zs", "b_hs")):
                eng.dma_start(out=ball[:, i * 4 + j:i * 4 + j + 1],
                              in_=bz[nm][0:1, j * 128:(j + 1) * 128].rearrange("o p -> p o")).then_inc(ds, 16)
                n += 1
        eng.dma_start(out=idb[:], in_=identb).then_inc(ds, 16)
        n += 1
        return n
    ld = p.stage("gpsimd", s_load, deps=[], dma=True)

    def s_stats(v):
        v.memset(zero1[:], 0.0)
        v.memset(epsc[:], EPS)
        v.tensor_scalar(nball[:], ball[:], -1.0, None, ALU.mult)
        ins = None
        for t in range(NT):
            v.tensor_reduce(stat[:, 0, t:t + 1], xs[:, t, :], AX.X, ALU.add)
            ins = v.scalar_tensor_tensor(sqs[:], xs[:, t, :], 1.0, xs[:, t, :],
                                         ALU.bypass, ALU.mult, accum_out=stat[:, 1, t:t + 1])
        p.chain("vector", v, ins)
        v.tensor_scalar(stat[:, 2, :], stat[:, 0, :], 1.0 / D, None, ALU.mult)
        ins = v.tensor_scalar(stat[:, 3, :], stat[:, 1, :], 1.0 / D, None, ALU.mult)
        p.chain("vector", v, ins)
        ins = v.scalar_tensor_tensor(stat[:, 0, :], stat[:, 2, :], 1.0, stat[:, 2, :],
                                     ALU.bypass, ALU.mult)
        p.chain("vector", v, ins)
        return v.tensor_sub(stat[:, 3, :], stat[:, 3, :], stat[:, 0, :])
    st_stats = p.stage("vector", s_stats, deps=[ld, ldx])

    def s_sqrt(sc):
        return sc.activation(stat[:, 4, :], stat[:, 3, :], AF.Sqrt, bias=epsc[:], scale=1.0)
    st_sqrt = p.stage("scalar", s_sqrt, deps=[st_stats])

    def s_recip(v):
        ins = v.reciprocal(stat[:, 3, :], stat[:, 4, :])
        p.chain("vector", v, ins)
        return v.scalar_tensor_tensor(stat[:, 0, :], stat[:, 2, :], -1.0,
                                      stat[:, 3, :], ALU.mult, ALU.mult)
    st_recip = p.stage("vector", s_recip, deps=[st_sqrt])

    def make_norm(t):
        def f(sc):
            return sc.activation(lnt[:, t % 2, :], xs[:, t, :], AF.Identity,
                                 bias=stat[:, 0, t:t + 1], scale=stat[:, 3, t:t + 1])
        return f

    def make_tp(t, c0, slot):
        def f(te):
            ins = None
            for i in range(4):
                ins = te.transpose(pstp[slot][:, i * 128:(i + 1) * 128],
                                   lnt[:, t % 2, (c0 + i) * 128:(c0 + i + 1) * 128], idb[:])
            return ins
        return f

    def make_tpe(t, c0, slot):
        def f(v):
            return v.tensor_copy(
                xT[:, c0:c0 + 4, t * 128:(t + 1) * 128],
                pstp[slot][:].rearrange("p (c q) -> p c q", c=4))
        return f

    prev_evac = {0: None, 1: None}
    prev_tp2 = {0: None, 1: None}
    all_tp = []
    gi = 0
    for t in range(NT):
        deps_n = [st_recip]
        if prev_tp2[t % 2] is not None:
            deps_n.append(prev_tp2[t % 2])
        sid_n = p.stage("scalar", make_norm(t), deps=deps_n)
        for c0 in (0, 4):
            slot = gi % 2
            deps = [sid_n]
            if prev_evac[slot] is not None:
                deps.append(prev_evac[slot])
            sid_t = p.stage("tensor", make_tp(t, c0, slot), deps=deps)
            sid_e = p.stage("vector", make_tpe(t, c0, slot), deps=[sid_t])
            prev_evac[slot] = sid_e
            prev_tp2[t % 2] = sid_t
            all_tp.append(sid_e)
            gi += 1

    def make_mmh(c, off, hf):
        j = c % 4

        def f(te):
            ins = None
            for n in (2 * hf, 2 * hf + 1):
                for k in range(DCH):
                    ins = te.matmul(pszb[:, n * 512:(n + 1) * 512],
                                    wall[:, k, off + j * 128: off + j * 128 + 128],
                                    xT[:, k, n * 512:(n + 1) * 512],
                                    start=(k == 0), stop=(k == DCH - 1))
            return ins
        return f

    def make_evz(c, hf):
        zb = (0 if c < 4 else 8) + (c % 4)
        cs = slice(hf * 1024, (hf + 1) * 1024)

        def f(sc):
            sc.activation(zt[:, c % 2, cs], pszb[:, cs], AF.Sigmoid,
                          bias=ball[:, zb:zb + 1], scale=1.0)
            return sc.activation(att[:, c % 2, cs], pszb[:, cs], AF.Sigmoid,
                                 bias=nball[:, zb:zb + 1], scale=-1.0)
        return f

    def make_evh(c, hf):
        hb = (4 if c < 4 else 12) + (c % 4)
        cs = slice(hf * 1024, (hf + 1) * 1024)

        def f(sc):
            return sc.activation(htt[:, c % 2, cs], pszb[:, cs], AF.Identity,
                                 bias=ball[:, hb:hb + 1], scale=1.0)
        return f

    def make_scan(c):
        def f(v):
            ins = v.tensor_tensor(asc[:], zt[:, c % 2, :], htt[:, c % 2, :], ALU.mult)
            p.chain("vector", v, ins)
            ins = v.tensor_tensor_scan(hsc[:], att[:, c % 2, :], asc[:], 0.0, ALU.mult, ALU.add)
            p.chain("vector", v, ins)
            ins = v.tensor_tensor_scan(asc[:], att[:, c % 2, :], zero1[:].broadcast_to([128, T]),
                                       1.0, ALU.mult, ALU.add)
            p.chain("vector", v, ins)
            v.tensor_copy(carrs[:, c:c + 1], asc[:, T - 1:T])
            return v.tensor_copy(carrs[:, 8 + c:9 + c], hsc[:, T - 1:T])
        return f

    def make_hout(c):
        @dma_stage(2)
        def f(eng, ds):
            eng.dma_start(out=h_out[c * 128:(c + 1) * 128, :], in_=hsc[:]).then_inc(ds, 16)
            eng.dma_start(out=a_out[c * 128:(c + 1) * 128, :], in_=asc[:]).then_inc(ds, 16)
            return 2
        return f

    hout_ids = []
    scans = {}
    prev_evh = {0: None, 1: None}
    for c in range(NSC):
        zoff = 0 if c < 4 else 1024
        hoff = 512 if c < 4 else 1536
        ez = {}
        eh = {}
        for hf in range(2):
            deps_m = list(all_tp)
            if prev_evh[hf] is not None:
                deps_m.append(prev_evh[hf])
            sid_m = p.stage("tensor", make_mmh(c, zoff, hf), deps=deps_m)
            deps_e = [sid_m]
            if c >= 2 and hf == 0:
                deps_e.append(scans[c - 2])
            ez[hf] = p.stage("scalar", make_evz(c, hf), deps=deps_e)
        for hf in range(2):
            sid_m = p.stage("tensor", make_mmh(c, hoff, hf), deps=[ez[hf]])
            deps_e = [sid_m]
            if c >= 2 and hf == 0:
                deps_e.append(scans[c - 2])
            eh[hf] = p.stage("scalar", make_evh(c, hf), deps=deps_e)
            prev_evh[hf] = eh[hf]
        deps_s = [ez[1], eh[1]]
        if c >= 1:
            deps_s.append(hout_ids[c - 1])
        sid_s = p.stage("vector", make_scan(c), deps=deps_s)
        scans[c] = sid_s
        sid_o = p.stage("gpsimd", make_hout(c), deps=[sid_s], dma=True)
        hout_ids.append(sid_o)

    @dma_stage(2 * NSC)
    def s_carr(eng, ds):
        for c in range(NSC):
            eng.dma_start(out=carr[0:1, c * 128:(c + 1) * 128].rearrange("o p -> p o"),
                          in_=carrs[:, c:c + 1]).then_inc(ds, 16)
            eng.dma_start(out=carr[1:2, c * 128:(c + 1) * 128].rearrange("o p -> p o"),
                          in_=carrs[:, 8 + c:9 + c]).then_inc(ds, 16)
        return 2 * NSC
    p.stage("gpsimd", s_carr, dma=True)
    p.stage("gpsimd", lambda g: g.engine_nop(), deps=None)

    p.build()
    return p


def build_phase2():
    p = Prog("p2")
    nc = p.nc
    x = nc.dram_tensor("x", [T, D], F32, kind="ExternalInput").ap()
    h_in = nc.dram_tensor("h_in", [D, T], F32, kind="ExternalInput").ap()
    a_in = nc.dram_tensor("a_in", [D, T], F32, kind="ExternalInput").ap()
    carrP = nc.dram_tensor("carrP", [8, D], F32, kind="ExternalInput").ap()
    carrH = nc.dram_tensor("carrH", [8, D], F32, kind="ExternalInput").ap()
    prevmask = nc.dram_tensor("prevmask", [128, 7], F32, kind="ExternalInput").ap()
    w_g = nc.dram_tensor("w_g", [D, D], F32, kind="ExternalInput").ap()
    b_g = nc.dram_tensor("b_g", [1, D], F32, kind="ExternalInput").ap()
    identb = nc.dram_tensor("identb", [128, 128], BF16, kind="ExternalInput").ap()
    identf = nc.dram_tensor("identf", [128, 128], F32, kind="ExternalInput").ap()
    x2o = nc.dram_tensor("x2", [T, D], F32, kind="ExternalOutput").ap()
    stats3 = nc.dram_tensor("stats3", [128, 2 * NT], F32, kind="ExternalOutput").ap()
    sxp = nc.dram_tensor("sxp", [1, 1], F32, kind="ExternalOutput").ap()

    es = ExitStack()
    p._es = es
    es.__enter__()
    sb = lambda name, shape, dt: es.enter_context(nc.sbuf_tensor(name, shape, dt))
    pst = lambda name, shape, dt: es.enter_context(nc.psum_tensor(name, shape, dt))

    hfm = sb("hfm", [128, NSC, T], F32)
    hbf = sb("hbf", [128, NSC, T], BF16)
    wg = sb("wg", [128, DCH, D], BF16)
    bgall = sb("bgall", [128, DCH], F32)
    cP = sb("cP", [8, D], F32)
    cH = sb("cH", [8, D], F32)
    msk = sb("msk", [128, 7], F32)
    omm = sb("omm", [128, 7], F32)
    PH = sb("PH", [128, NSC, 16], F32)
    Hc = sb("Hc", [128, NSC], F32)
    Qj = sb("Qj", [128, NSC], F32)
    Gj = sb("Gj", [128, NSC], F32)
    sio = sb("sio", [128, 4, T], F32)
    gsg = sb("gsg", [128, 2, T], F32)
    att_t = sb("att_t", [128, 2, D], F32)
    xtl = sb("xtl", [128, 2, D], F32)
    x2t = sb("x2t", [128, 2, D], F32)
    xl3 = sb("xl3", [128, 2, D], F32)
    sqs = sb("sqs", [128, D], F32)
    stc = sb("stc", [128, 2, 8], F32)
    m3s = sb("m3s", [128, NT], F32)
    r3s = sb("r3s", [128, NT], F32)
    axc = sb("axc", [128, 2], F32)
    axr = sb("axr", [128, 2], F32)
    axf = sb("axf", [128, 1], F32)
    epsc = sb("epsc", [128, 1], F32)
    idb = sb("idb", [128, 128], BF16)
    idf = sb("idf", [128, 128], F32)

    psG = [pst(f"psG{i}", [128, 1024], F32) for i in range(2)]
    psT = [pst(f"psT{i}", [128, 1024], F32) for i in range(2)]

    @dma_stage(4 + DCH + 1 + DCH)
    def s_load(eng, ds):
        n = 0
        eng.dma_start(out=cP[:], in_=carrP).then_inc(ds, 16); n += 1
        eng.dma_start(out=cH[:], in_=carrH).then_inc(ds, 16); n += 1
        eng.dma_start(out=msk[:], in_=prevmask).then_inc(ds, 16); n += 1
        eng.dma_start(out=idb[:], in_=identb).then_inc(ds, 16); n += 1
        for k in range(DCH):
            eng.dma_start(out=wg[:, k, :], in_=w_g[k * 128:(k + 1) * 128, :]).then_inc(ds, 16)
            n += 1
        eng.dma_start(out=idf[:], in_=identf).then_inc(ds, 16); n += 1
        for j in range(DCH):
            eng.dma_start(out=bgall[:, j:j + 1],
                          in_=b_g[0:1, j * 128:(j + 1) * 128].rearrange("o p -> p o")).then_inc(ds, 16)
            n += 1
        return n
    ld = p.stage("gpsimd", s_load, deps=[], dma=True)

    def s_prep(v):
        v.memset(epsc[:], EPS)
        v.memset(Hc[:], 0.0)
        v.memset(axc[:], 0.0)
        return v.tensor_scalar(omm[:], msk[:], -1.0, 1.0, ALU.mult, ALU.add)
    st_prep = p.stage("vector", s_prep, deps=[ld])

    def make_ctp(r):
        def f(te):
            ins = None
            for q in range(4):
                c = r * 4 + q
                te.transpose(psT[r % 2][:, q * 256:q * 256 + 8],
                             cP[:, c * 128:(c + 1) * 128], idf[0:8, 0:8])
                ins = te.transpose(psT[r % 2][:, q * 256 + 8:q * 256 + 16],
                                   cH[:, c * 128:(c + 1) * 128], idf[0:8, 0:8])
            return ins
        return f

    def make_ctpe(r):
        def f(v):
            ins = None
            for q in range(4):
                c = r * 4 + q
                ins = v.tensor_copy(PH[:, c, :], psT[r % 2][:, q * 256:q * 256 + 16])
            return ins
        return f

    ctp_done = None
    for r in range(2):
        sid_t = p.stage("tensor", make_ctp(r), deps=[ld])
        ctp_done = p.stage("vector", make_ctpe(r), deps=[sid_t])

    def s_fold(v):
        ins = None
        for j in range(7):
            v.tensor_scalar(Qj[:], PH[:, :, j:j + 1].squeeze(2), msk[:, j:j + 1],
                            omm[:, j:j + 1], ALU.mult, ALU.add)
            ins = v.tensor_scalar(Gj[:], PH[:, :, 8 + j:9 + j].squeeze(2), msk[:, j:j + 1],
                                  None, ALU.mult)
            p.chain("vector", v, ins)
            ins = v.tensor_tensor(Hc[:], Hc[:], Qj[:], ALU.mult)
            p.chain("vector", v, ins)
            ins = v.tensor_tensor(Hc[:], Hc[:], Gj[:], ALU.add)
            p.chain("vector", v, ins)
        return v.engine_nop()
    st_fold = p.stage("vector", s_fold, deps=[st_prep, ctp_done])

    def make_hload2(g):
        @dma_stage(2)
        def f(eng, ds):
            c0 = g * 2
            base = (g % 2) * 2
            eng.dma_start(out=sio[:, base:base + 2, :],
                          in_=h_in[c0 * 128:(c0 + 2) * 128, :].rearrange("(s q) c -> q s c", q=128)).then_inc(ds, 16)
            eng.dma_start(out=hfm[:, c0:c0 + 2, :],
                          in_=a_in[c0 * 128:(c0 + 2) * 128, :].rearrange("(s q) c -> q s c", q=128)).then_inc(ds, 16)
            return 2
        return f

    def make_fix(g):
        def f(v):
            ins = None
            for j in range(2):
                c = g * 2 + j
                base = (g % 2) * 2
                ins = v.scalar_tensor_tensor(hfm[:, c, :], hfm[:, c, :], Hc[:, c:c + 1],
                                             sio[:, base + j, :], ALU.mult, ALU.add)
            p.chain("vector", v, ins)
            for j in range(2):
                c = g * 2 + j
                ins = v.tensor_copy(hbf[:, c, :], hfm[:, c, :])
            return ins
        return f

    prev_fix = None
    fix_done = None
    for g in range(NSC // 2):
        deps_l = []
        if prev_fix is not None:
            deps_l.append(prev_fix)
        sid_l = p.stage("sync", make_hload2(g), deps=deps_l, dma=True)
        prev_fix = p.stage("vector", make_fix(g), deps=[sid_l, st_fold])
    fix_done = prev_fix

    def make_gmm(g, hh):
        def f(te):
            ins = None
            for n in (hh * 2, hh * 2 + 1):
                for k in range(DCH):
                    ins = te.matmul(psG[hh][:, (n % 2) * 512:(n % 2) * 512 + 512],
                                    wg[:, k, g * 128:(g + 1) * 128],
                                    hbf[:, k, n * 512:(n + 1) * 512],
                                    start=(k == 0), stop=(k == DCH - 1))
            return ins
        return f

    def make_gevac(g, hh):
        def f(sc):
            return sc.activation(gsg[:, g % 2, hh * 1024:(hh + 1) * 1024], psG[hh][:],
                                 AF.Sigmoid, bias=bgall[:, g:g + 1], scale=1.0)
        return f

    def make_gmul(g):
        def f(gp):
            return gp.tensor_tensor(hfm[:, g, :], gsg[:, g % 2, :], hfm[:, g, :], ALU.mult)
        return f

    prev_ev = {0: None, 1: None}
    prev_mul = {}
    for g in range(NSC):
        evs = []
        for hh in range(2):
            deps_m = [fix_done]
            if prev_ev[hh] is not None:
                deps_m.append(prev_ev[hh])
            sid_m = p.stage("tensor", make_gmm(g, hh), deps=deps_m)
            deps_e = [sid_m]
            if g >= 2 and (g - 2) in prev_mul:
                deps_e.append(prev_mul[g - 2])
            sid_e = p.stage("scalar", make_gevac(g, hh), deps=deps_e)
            prev_ev[hh] = sid_e
            evs.append(sid_e)
        sid_u = p.stage("gpsimd", make_gmul(g), deps=evs)
        prev_mul[g] = sid_u
    attn_done = prev_mul[NSC - 1]

    def make_xload(t):
        @dma_stage(1)
        def f(eng, ds):
            eng.dma_start(out=xtl[:, t % 2, :], in_=x[t * 128:(t + 1) * 128, :]).then_inc(ds, 16)
            return 1
        return f

    def make_attp(t):
        def f(te):
            ins = None
            for c in range(NSC):
                ins = te.matmul(psT[t % 2][:, c * 128:(c + 1) * 128],
                                hfm[:, c, t * 128:(t + 1) * 128], idf[:],
                                start=True, stop=True)
            return ins
        return f

    def make_attpe(t):
        def f(s):
            return s.activation(att_t[:, t % 2, :], psT[t % 2][:], AF.Identity,
                                bias=0.0, scale=1.0, accum_out=stc[:, t % 2, 0:1])
        return f

    def make_ln2a(t):
        def f(v):
            sl = t % 2
            ins = v.scalar_tensor_tensor(sqs[:], att_t[:, sl, :], 1.0, att_t[:, sl, :],
                                         ALU.bypass, ALU.mult, accum_out=stc[:, sl, 1:2])
            p.chain("vector", v, ins)
            v.tensor_scalar(stc[:, sl, 2:3], stc[:, sl, 0:1], 1.0 / D, None, ALU.mult)
            ins = v.tensor_scalar(stc[:, sl, 3:4], stc[:, sl, 1:2], 1.0 / D, None, ALU.mult)
            p.chain("vector", v, ins)
            ins = v.scalar_tensor_tensor(stc[:, sl, 0:1], stc[:, sl, 2:3], 1.0, stc[:, sl, 2:3],
                                         ALU.bypass, ALU.mult)
            p.chain("vector", v, ins)
            return v.tensor_sub(stc[:, sl, 3:4], stc[:, sl, 3:4], stc[:, sl, 0:1])
        return f

    def make_ln2b(t):
        def f(s):
            sl = t % 2
            return s.activation(stc[:, sl, 4:5], stc[:, sl, 3:4], AF.Sqrt,
                                bias=epsc[:], scale=1.0)
        return f

    def make_ln2c(t):
        def f(v):
            sl = t % 2
            ins = v.reciprocal(stc[:, sl, 5:6], stc[:, sl, 4:5])
            p.chain("vector", v, ins)
            return v.scalar_tensor_tensor(stc[:, sl, 6:7], stc[:, sl, 2:3], -1.0,
                                          stc[:, sl, 5:6], ALU.mult, ALU.mult)
        return f

    def make_ln2d(t):
        def f(s):
            sl = t % 2
            return s.activation(xl3[:, sl, :], att_t[:, sl, :], AF.Identity,
                                bias=stc[:, sl, 6:7], scale=stc[:, sl, 5:6])
        return f

    def make_resid(t):
        def f(v):
            sl = t % 2
            ins = v.scalar_tensor_tensor(x2t[:, sl, :], xtl[:, sl, :], 1.0, xl3[:, sl, :],
                                         ALU.bypass, ALU.add, accum_out=stc[:, sl, 0:1])
            p.chain("vector", v, ins)
            ins = v.scalar_tensor_tensor(sqs[:], x2t[:, sl, :], 1.0, x2t[:, sl, :],
                                         ALU.bypass, ALU.mult, accum_out=stc[:, sl, 1:2])
            p.chain("vector", v, ins)
            v.tensor_scalar(stc[:, sl, 2:3], stc[:, sl, 0:1], 1.0 / D, None, ALU.mult)
            ins = v.tensor_scalar(stc[:, sl, 3:4], stc[:, sl, 1:2], 1.0 / D, None, ALU.mult)
            p.chain("vector", v, ins)
            ins = v.scalar_tensor_tensor(stc[:, sl, 0:1], stc[:, sl, 2:3], 1.0, stc[:, sl, 2:3],
                                         ALU.bypass, ALU.mult)
            p.chain("vector", v, ins)
            return v.tensor_sub(stc[:, sl, 3:4], stc[:, sl, 3:4], stc[:, sl, 0:1])
        return f

    def make_ln3b(t):
        def f(s):
            sl = t % 2
            return s.activation(stc[:, sl, 4:5], stc[:, sl, 3:4], AF.Sqrt,
                                bias=epsc[:], scale=1.0)
        return f

    def make_ln3c(t):
        def f(v):
            sl = t % 2
            v.reciprocal(r3s[:, t:t + 1], stc[:, sl, 4:5])
            ins = v.tensor_copy(m3s[:, t:t + 1], stc[:, sl, 2:3])
            p.chain("vector", v, ins)
            return v.scalar_tensor_tensor(stc[:, sl, 6:7], stc[:, sl, 2:3], -1.0,
                                          r3s[:, t:t + 1], ALU.mult, ALU.mult)
        return f

    def make_ln3d(t):
        def f(s):
            sl = t % 2
            return s.activation(xl3[:, sl, :], x2t[:, sl, :], AF.Identity,
                                bias=stc[:, sl, 6:7], scale=r3s[:, t:t + 1])
        return f

    def make_sx(t):
        def f(v):
            sl = t % 2
            ins = v.tensor_reduce(axr[:, sl:sl + 1], xl3[:, sl, :], AX.X, ALU.max,
                                  apply_absolute_value=True)
            p.chain("vector", v, ins)
            return v.tensor_tensor(axc[:, sl:sl + 1], axc[:, sl:sl + 1],
                                   axr[:, sl:sl + 1], ALU.max)
        return f

    def make_x2out(t):
        @dma_stage(1)
        def f(eng, ds):
            eng.dma_start(out=x2o[t * 128:(t + 1) * 128, :], in_=x2t[:, t % 2, :]).then_inc(ds, 16)
            return 1
        return f

    prev_tile = {}

    def front2(t):
        deps_x = [attn_done] if t < 2 else [attn_done, prev_tile[(t - 2, "resid")],
                                            prev_tile[(t - 2, "out")]]
        sid_xl = p.stage("sync", make_xload(t), deps=deps_x, dma=True)
        deps_tp = [attn_done]
        if t >= 2:
            deps_tp.append(prev_tile[(t - 2, "tpe")])
        sid_tp = p.stage("tensor", make_attp(t), deps=deps_tp)
        deps_te = [sid_tp]
        if t >= 2:
            deps_te.append(prev_tile[(t - 2, "ln2d")])
            deps_te.append(prev_tile[(t - 2, "resid")])
        sid_te = p.stage("scalar", make_attpe(t), deps=deps_te)
        sid_a = p.stage("vector", make_ln2a(t), deps=[sid_te])
        sid_b = p.stage("scalar", make_ln2b(t), deps=[sid_a])
        prev_tile[(t, "xl")] = sid_xl
        prev_tile[(t, "tpe")] = sid_te
        prev_tile[(t, "ln2b")] = sid_b

    def back2(t):
        sid_c = p.stage("vector", make_ln2c(t), deps=[prev_tile[(t, "ln2b")]])
        deps_d = [sid_c]
        if t >= 2:
            deps_d.append(prev_tile[(t - 2, "resid")])
        sid_d = p.stage("scalar", make_ln2d(t), deps=deps_d)
        deps_r = [sid_d, prev_tile[(t, "xl")]]
        if t >= 2:
            deps_r.append(prev_tile[(t - 2, "out")])
        sid_r = p.stage("vector", make_resid(t), deps=deps_r)
        sid_b3 = p.stage("scalar", make_ln3b(t), deps=[sid_r])
        sid_c3 = p.stage("vector", make_ln3c(t), deps=[sid_b3])
        deps_d3 = [sid_c3]
        if t >= 2:
            deps_d3.append(prev_tile[(t - 2, "sx")])
        sid_d3 = p.stage("scalar", make_ln3d(t), deps=deps_d3)
        sid_sx = p.stage("vector", make_sx(t), deps=[sid_d3])
        sid_o = p.stage("gpsimd", make_x2out(t), deps=[sid_r], dma=True)
        prev_tile[(t, "resid")] = sid_r
        prev_tile[(t, "ln2d")] = sid_d
        prev_tile[(t, "sx")] = sid_sx
        prev_tile[(t, "out")] = sid_o

    for t in range(NT + 1):
        if t < NT:
            front2(t)
        if t >= 1:
            back2(t - 1)

    def s_sxv(v):
        v.tensor_reduce(axf[:], axc[:], AX.X, ALU.max)
        return v.memset(stc[:, 0, 7:8], 0.0)
    p.stage("vector", s_sxv, deps=None)

    def s_sxred(g):
        return g.tensor_reduce(stc[0:1, 0, 7:8], axf[:], AX.C, ALU.max)
    p.stage("gpsimd", s_sxred, deps=None)

    @dma_stage(3)
    def s_souts(eng, ds):
        eng.dma_start(out=stats3[:, 0:NT], in_=m3s[:]).then_inc(ds, 16)
        eng.dma_start(out=stats3[:, NT:2 * NT], in_=r3s[:]).then_inc(ds, 16)
        eng.dma_start(out=sxp, in_=stc[0:1, 0, 7:8]).then_inc(ds, 16)
        return 3
    p.stage("gpsimd", s_souts, deps=None, dma=True)
    p.stage("gpsimd", lambda g: g.engine_nop(), deps=None)

    p.build()
    return p


def build_phase3():
    p = Prog("p3")
    nc = p.nc
    x2 = nc.dram_tensor("x2", [T, D], F32, kind="ExternalInput").ap()
    stats3t = nc.dram_tensor("stats3t", [128, 2 * NT], F32, kind="ExternalInput").ap()
    sxall = nc.dram_tensor("sxall", [128, 8], F32, kind="ExternalInput").ap()
    w_up = nc.dram_tensor("w_up", [FFI, D], F32, kind="ExternalInput").ap()
    w_gate = nc.dram_tensor("w_gate", [FFI, D], F32, kind="ExternalInput").ap()
    identb = nc.dram_tensor("identb", [128, 128], BF16, kind="ExternalInput").ap()
    up_out = nc.dram_tensor("up_out", [FFI, T], F32, kind="ExternalOutput").ap()
    gt_out = nc.dram_tensor("gt_out", [FFI, T], F32, kind="ExternalOutput").ap()
    mxp = nc.dram_tensor("mxp", [1, 2], F32, kind="ExternalOutput").ap()

    es = ExitStack()
    p._es = es
    es.__enter__()
    sb = lambda name, shape, dt: es.enter_context(nc.sbuf_tensor(name, shape, dt))
    pst = lambda name, shape, dt: es.enter_context(nc.psum_tensor(name, shape, dt))

    xqT = sb("xqT", [128, DCH, T], BF16)
    wTu = sb("wTu", [128, DCH, FFI], BF16)
    wTg = sb("wTg", [128, DCH, FFI], BF16)
    s3 = sb("s3", [128, 2 * NT], F32)
    sxa = sb("sxa", [128, 8], F32)
    sxc = sb("sxc", [128, 4], F32)
    rsb = sb("rsb", [128, NT], F32)
    bsb = sb("bsb", [128, NT], F32)
    x2t = sb("x2t", [128, 4, D], F32)
    uq = sb("uq", [128, 2, D], F32)
    dqi = sb("dqi", [128, 2, D], I16)
    xqtile = sb("xqtile", [128, 2, D], BF16)
    wstr = sb("wstr", [128, 4, D], F32)
    bm = sb("bm", [128, 2, 32], F32)
    sbc = sb("sbc", [128, 2, 32], F32)
    sbi = sb("sbi", [128, 2, 32], F32)
    d8 = sb("d8", [128, 2, D], I8)
    wqb = sb("wqb", [128, 2, D], BF16)
    upc = sb("upc", [128, 2, T], F32)
    acc = sb("acc", [128, 2], F32)
    axr = sb("axr", [128, 2], F32)
    redo = sb("redo", [1, 2], F32)
    idb = sb("idb", [128, 128], BF16)

    psA = pst("psA", [128, 2048], F32)
    psB = pst("psB", [128, 2048], F32)
    PS = [psA, psB]

    @dma_stage(3)
    def s_load(eng, ds):
        eng.dma_start(out=s3[:], in_=stats3t).then_inc(ds, 16)
        eng.dma_start(out=sxa[:], in_=sxall).then_inc(ds, 16)
        eng.dma_start(out=idb[:], in_=identb).then_inc(ds, 16)
        return 3
    ld = p.stage("gpsimd", s_load, deps=[], dma=True)

    def s_sx(v):
        v.memset(acc[:], 0.0)
        ins = v.tensor_reduce(sxc[:, 0:1], sxa[:], AX.X, ALU.max)
        p.chain("vector", v, ins)
        ins = v.tensor_scalar(sxc[:, 1:2], sxc[:, 0:1], 1.0 / QA, 1e-12, ALU.mult, ALU.max)
        p.chain("vector", v, ins)
        ins = v.reciprocal(sxc[:, 2:3], sxc[:, 1:2])
        p.chain("vector", v, ins)
        ins = v.tensor_scalar(rsb[:], s3[:, NT:2 * NT], sxc[:, 2:3], None, ALU.mult)
        p.chain("vector", v, ins)
        return v.scalar_tensor_tensor(bsb[:], s3[:, 0:NT], -1.0, rsb[:],
                                      ALU.mult, ALU.mult)
    st_sx = p.stage("vector", s_sx, deps=[ld])

    def make_x2l(g):
        @dma_stage(1)
        def f(eng, ds):
            t0 = 2 * g
            sl0 = t0 % 4
            eng.dma_start(out=x2t[:, sl0:sl0 + 2, :],
                          in_=x2[t0 * 128:(t0 + 2) * 128, :].rearrange("(s q) c -> q s c", q=128)).then_inc(ds, 16)
            return 1
        return f

    def make_xqb(g):
        def f(s):
            ins = None
            for t in (2 * g, 2 * g + 1):
                ins = s.activation(uq[:, t % 2, :], x2t[:, t % 4, :], AF.Identity,
                                   bias=bsb[:, t:t + 1], scale=rsb[:, t:t + 1])
            return ins
        return f

    def make_xqc(g):
        def f(v):
            ins = None
            for t in (2 * g, 2 * g + 1):
                ins = v.tensor_scalar(dqi[:, t % 2, :], uq[:, t % 2, :], 1.0, None, ALU.mult)
            p.chain("vector", v, ins)
            for t in (2 * g, 2 * g + 1):
                ins = v.tensor_scalar(xqtile[:, t % 2, :], dqi[:, t % 2, :],
                                      sxc[:, 1:2], None, ALU.mult)
            return ins
        return f

    def make_xtp(g):
        def f(te):
            ins = None
            for t in (2 * g, 2 * g + 1):
                for c in range(DCH):
                    ins = te.matmul(PS[g % 2][:, (t % 2) * 1024 + c * 128:(t % 2) * 1024 + (c + 1) * 128],
                                    xqtile[:, t % 2, c * 128:(c + 1) * 128], idb[:],
                                    start=True, stop=True)
            return ins
        return f

    def make_xtpe(g):
        def f(s):
            ins = None
            for t in (2 * g, 2 * g + 1):
                ins = s.activation(
                    xqT[:, :, t * 128:(t + 1) * 128],
                    PS[g % 2][:, (t % 2) * 1024:(t % 2 + 1) * 1024].rearrange(
                        "p (c q) -> p c q", c=DCH),
                    AF.Identity, bias=0.0, scale=1.0)
            return ins
        return f

    prev_ps = {0: None, 1: None}
    xq_done = []
    NG = NT // 2
    xst = {}

    def emit_xq(g):
        deps_l = [st_sx]
        if g >= 2:
            deps_l.append(xst[(g - 2, "e")])
        sid_l = p.stage("sync", make_x2l(g), deps=deps_l, dma=True)
        deps_b = [sid_l, st_sx]
        if g >= 1:
            deps_b.append(xst[(g - 1, "t")])
        sid_b = p.stage("scalar", make_xqb(g), deps=deps_b)
        xst[(g, "c")] = p.stage("vector", make_xqc(g), deps=[sid_b])

    emit_xq(0)
    for g in range(NG):
        deps_t = [xst[(g, "c")]]
        if prev_ps[g % 2] is not None:
            deps_t.append(prev_ps[g % 2])
        sid_t = p.stage("tensor", make_xtp(g), deps=deps_t)
        xst[(g, "t")] = sid_t
        sid_e = p.stage("scalar", make_xtpe(g), deps=[sid_t])
        prev_ps[g % 2] = sid_e
        xst[(g, "e")] = sid_e
        xq_done.append(sid_e)
        if g + 1 < NG:
            emit_xq(g + 1)

    def make_wl(src_w, g):
        @dma_stage(1)
        def f(eng, ds):
            i0 = 2 * g
            sl0 = i0 % 4
            eng.dma_start(out=wstr[:, sl0:sl0 + 2, :],
                          in_=src_w[i0 * 128:(i0 + 2) * 128, :].rearrange("(s q) c -> q s c", q=128)).then_inc(ds, 16)
            return 1
        return f

    def make_wdqa(g):
        def f(v):
            ins = None
            for i in (2 * g, 2 * g + 1):
                ins = v.tensor_reduce(bm[:, i % 2, :], wstr[:, i % 4, :].rearrange("p (b q) -> p b q", q=32),
                                      AX.X, ALU.max, apply_absolute_value=True)
            p.chain("vector", v, ins)
            for i in (2 * g, 2 * g + 1):
                ins = v.tensor_scalar(sbc[:, i % 2, :], bm[:, i % 2, :], 1.0 / QW, 1e-12,
                                      ALU.mult, ALU.max)
            p.chain("vector", v, ins)
            for i in (2 * g, 2 * g + 1):
                ins = v.reciprocal(sbi[:, i % 2, :], sbc[:, i % 2, :])
            return ins
        return f

    def make_wdq2(g):
        def f(v):
            ins = None
            for i in (2 * g, 2 * g + 1):
                ins = v.tensor_tensor(d8[:, i % 2, :].rearrange("p (b q) -> p b q", q=32),
                                      wstr[:, i % 4, :].rearrange("p (b q) -> p b q", q=32),
                                      sbi[:, i % 2, :].unsqueeze(2).broadcast_to([128, 32, 32]),
                                      ALU.mult)
            p.chain("vector", v, ins)
            for i in (2 * g, 2 * g + 1):
                ins = v.tensor_tensor(wqb[:, i % 2, :].rearrange("p (b q) -> p b q", q=32),
                                      d8[:, i % 2, :].rearrange("p (b q) -> p b q", q=32),
                                      sbc[:, i % 2, :].unsqueeze(2).broadcast_to([128, 32, 32]),
                                      ALU.mult)
            return ins
        return f

    def make_wtp(g, slot):
        def f(te):
            ins = None
            for i in (2 * g, 2 * g + 1):
                for c in range(DCH):
                    ins = te.matmul(PS[slot][:, (i % 2) * 1024 + c * 128:(i % 2) * 1024 + (c + 1) * 128],
                                    wqb[:, i % 2, c * 128:(c + 1) * 128], idb[:],
                                    start=True, stop=True)
            return ins
        return f

    def make_wtpe(dst, g, slot):
        def f(s):
            ins = None
            for i in (2 * g, 2 * g + 1):
                ins = s.activation(
                    dst[:, :, i * 128:(i + 1) * 128],
                    PS[slot][:, (i % 2) * 1024:(i % 2 + 1) * 1024].rearrange(
                        "p (c q) -> p c q", c=DCH),
                    AF.Identity, bias=0.0, scale=1.0)
            return ins
        return f

    def make_mm(wT, i, ps):
        def f(te):
            ins = None
            for n in range(4):
                for k in range(DCH):
                    ins = te.matmul(ps[:, n * 512:(n + 1) * 512],
                                    wT[:, k, i * 128:(i + 1) * 128],
                                    xqT[:, k, n * 512:(n + 1) * 512],
                                    start=(k == 0), stop=(k == DCH - 1))
            return ins
        return f

    def make_mev(i, ps):
        def f(s):
            return s.activation(upc[:, i % 2, :], ps[:], AF.Identity, bias=0.0, scale=1.0)
        return f

    def make_max(i, a):
        def f(v):
            ins = v.tensor_reduce(axr[:, a:a + 1], upc[:, i % 2, :], AX.X, ALU.max,
                                  apply_absolute_value=True)
            p.chain("vector", v, ins)
            return v.tensor_tensor(acc[:, a:a + 1], acc[:, a:a + 1], axr[:, a:a + 1], ALU.max)
        return f

    def make_mout(dst, i0):
        @dma_stage(1)
        def f(eng, ds):
            eng.dma_start(out=dst[i0 * 128:(i0 + 2) * 128, :].rearrange("(s q) c -> q s c", q=128),
                          in_=upc[:, 0:2, :]).then_inc(ds, 16)
            return 1
        return f

    LAG = 1
    NWG = FCH // 2
    TWG = 2 * NWG
    fam = ((w_up, wTu, up_out), (w_gate, wTg, gt_out))
    rot = {"pu": NG}
    last_evac = dict(prev_ps)
    w_evac = {}
    prev_wt = {}
    prev_wb = {}
    prev_wd = {}
    mmstate = {"out": None, "max": {0: None, 1: None}}

    def emit_mm(m):
        wi, gl = divmod(m, NWG)
        wT, dst = fam[wi][1], fam[wi][2]
        evs = []
        for j in range(2):
            i = 2 * gl + j
            slot = rot["pu"] % 2
            rot["pu"] += 1
            deps_m = [xq_done[-1], w_evac[m]]
            if last_evac[slot] is not None:
                deps_m.append(last_evac[slot])
            sid_m = p.stage("tensor", make_mm(wT, i, PS[slot]), deps=deps_m)
            deps_e = [sid_m]
            if mmstate["out"] is not None:
                deps_e.append(mmstate["out"])
            if mmstate["max"][j] is not None:
                deps_e.append(mmstate["max"][j])
            sid_e = p.stage("scalar", make_mev(i, PS[slot]), deps=deps_e)
            last_evac[slot] = sid_e
            sid_x = p.stage("vector", make_max(i, wi), deps=[sid_e])
            mmstate["max"][j] = sid_x
            evs.append(sid_e)
        sid_o = p.stage("gpsimd", make_mout(dst, 2 * gl), deps=evs, dma=True)
        mmstate["out"] = sid_o

    dq_done = {}

    def emit_dequant(gg):
        wi, gl = divmod(gg, NWG)
        deps_l = []
        if gg >= 2:
            deps_l.append(prev_wb[gg - 2])
        sid_l = p.stage("sync" if gg % 2 == 0 else "gpsimd",
                        make_wl(fam[wi][0], gl), deps=deps_l, dma=True)
        sid_a = p.stage("vector", make_wdqa(gl), deps=[sid_l])
        deps_d = [sid_a]
        if gg >= 1:
            deps_d.append(prev_wt[gg - 1])
        sid_d = p.stage("vector", make_wdq2(gl), deps=deps_d)
        prev_wb[gg] = sid_d
        dq_done[gg] = sid_d

    emit_dequant(0)
    for gg in range(TWG + LAG):
        if gg < TWG:
            wi, gl = divmod(gg, NWG)
            dst_T = fam[wi][1]
            slot = rot["pu"] % 2
            rot["pu"] += 1
            deps_t = [dq_done[gg]]
            if last_evac[slot] is not None:
                deps_t.append(last_evac[slot])
            sid_t = p.stage("tensor", make_wtp(gl, slot), deps=deps_t)
            prev_wt[gg] = sid_t
            sid_e = p.stage("scalar", make_wtpe(dst_T, gl, slot), deps=[sid_t])
            last_evac[slot] = sid_e
            w_evac[gg] = sid_e
            if gg + 1 < TWG:
                emit_dequant(gg + 1)
        if gg >= LAG:
            emit_mm(gg - LAG)

    def s_fin(g):
        g.tensor_reduce(redo[0:1, 0:1], acc[:, 0:1], AX.C, ALU.max)
        return g.tensor_reduce(redo[0:1, 1:2], acc[:, 1:2], AX.C, ALU.max)
    p.stage("gpsimd", s_fin, deps=None)

    @dma_stage(1)
    def s_mxo(eng, ds):
        eng.dma_start(out=mxp, in_=redo[0:1, :]).then_inc(ds, 16)
        return 1
    p.stage("gpsimd", s_mxo, deps=None, dma=True)
    p.stage("gpsimd", lambda g: g.engine_nop(), deps=None)

    p.build()
    return p


def build_phase4():
    p = Prog("p4")
    nc = p.nc
    up_in = nc.dram_tensor("up_in", [FFI, T], F32, kind="ExternalInput").ap()
    gt_in = nc.dram_tensor("gt_in", [FFI, T], F32, kind="ExternalInput").ap()
    mxall = nc.dram_tensor("mxall", [128, 16], F32, kind="ExternalInput").ap()
    d_up = nc.dram_tensor("d_up", [FFI, T], I16, kind="ExternalOutput").ap()
    d_gt = nc.dram_tensor("d_gt", [FFI, T], I16, kind="ExternalOutput").ap()
    ghp = nc.dram_tensor("ghp", [1, 1], F32, kind="ExternalOutput").ap()
    scl = nc.dram_tensor("scl", [1, 2], F32, kind="ExternalOutput").ap()

    es = ExitStack()
    p._es = es
    es.__enter__()
    sb = lambda name, shape, dt: es.enter_context(nc.sbuf_tensor(name, shape, dt))

    G = 2
    ups = sb("ups", [128, 2 * G, T], F32)
    gts = sb("gts", [128, 2 * G, T], F32)
    dut = sb("dut", [128, 2 * G, T], I16)
    dgt = sb("dgt", [128, 2 * G, T], I16)
    sig = sb("sig", [128, 2 * G, T], F32)
    d2t = sb("d2t", [128, 2 * G, T], I32)
    ght = sb("ght", [128, 2 * G, T], F32)
    mxs = sb("mxs", [128, 16], F32)
    sc = sb("sc", [128, 6], F32)
    axr = sb("axr", [128, 2 * G], F32)
    acc = sb("acc", [128, 2 * G], F32)
    outs = sb("outs", [1, 3], F32)
    accf = sb("accf", [128, 1], F32)

    @dma_stage(1)
    def s_load(eng, ds):
        eng.dma_start(out=mxs[:], in_=mxall).then_inc(ds, 16)
        return 1
    ld = p.stage("gpsimd", s_load, deps=[], dma=True)

    def s_scales(v):
        v.memset(acc[:], 0.0)
        v.tensor_reduce(sc[:, 0:1], mxs[:, 0:8], AX.X, ALU.max)
        ins = v.tensor_reduce(sc[:, 3:4], mxs[:, 8:16], AX.X, ALU.max)
        p.chain("vector", v, ins)
        v.tensor_scalar(sc[:, 1:2], sc[:, 0:1], 1.0 / QA, 1e-12, ALU.mult, ALU.max)
        ins = v.tensor_scalar(sc[:, 4:5], sc[:, 3:4], 1.0 / QA, 1e-12, ALU.mult, ALU.max)
        p.chain("vector", v, ins)
        v.reciprocal(sc[:, 2:3], sc[:, 1:2])
        return v.reciprocal(sc[:, 5:6], sc[:, 4:5])
    st_sc = p.stage("vector", s_scales, deps=[ld])

    NL = 2 * G

    def make_l(i):
        @dma_stage(2)
        def f(eng, ds):
            sl = i % NL
            eng.dma_start(out=ups[:, sl, :],
                          in_=up_in[i * 128:(i + 1) * 128, :]).then_inc(ds, 16)
            eng.dma_start(out=gts[:, sl, :],
                          in_=gt_in[i * 128:(i + 1) * 128, :]).then_inc(ds, 16)
            return 2
        return f

    def make_q(i):
        def f(v):
            sl = i % NL
            v.tensor_scalar(dut[:, sl, :], ups[:, sl, :], sc[:, 2:3], None, ALU.mult)
            return v.tensor_scalar(dgt[:, sl, :], gts[:, sl, :], sc[:, 5:6], None, ALU.mult)
        return f

    def make_o(i):
        @dma_stage(2)
        def f(eng, ds):
            sl = i % NL
            eng.dma_start(out=d_up[i * 128:(i + 1) * 128, :],
                          in_=dut[:, sl, :]).then_inc(ds, 16)
            eng.dma_start(out=d_gt[i * 128:(i + 1) * 128, :],
                          in_=dgt[:, sl, :]).then_inc(ds, 16)
            return 2
        return f

    def make_sig(i):
        def f(s):
            sl = i % NL
            return s.activation(sig[:, sl, :], dgt[:, sl, :], AF.Sigmoid,
                                bias=0.0, scale=sc[:, 4:5])
        return f

    def make_d2(i):
        def f(v):
            sl = i % NL
            return v.tensor_scalar(d2t[:, sl, :], sig[:, sl, :], 65536.0, 65535.49,
                                   ALU.mult, ALU.min)
        return f

    def make_gh(i):
        def f(gp):
            sl = i % NL
            return gp.tensor_tensor(ght[:, sl, :], dgt[:, sl, :], d2t[:, sl, :], ALU.mult)
        return f

    def make_mx(i):
        def f(v):
            sl = i % NL
            ins = v.tensor_reduce(axr[:, sl:sl + 1], ght[:, sl, :], AX.X, ALU.max,
                                  apply_absolute_value=True)
            p.chain("vector", v, ins)
            return v.tensor_tensor(acc[:, sl:sl + 1], acc[:, sl:sl + 1],
                                   axr[:, sl:sl + 1], ALU.max)
        return f

    st = {}

    def emit4(step):
        i = step
        if i < FCH:
            deps_l = [st_sc]
            if i >= NL:
                deps_l.append(st[(i - NL, "q")])
            st[(i, "l")] = p.stage("sync", make_l(i), deps=deps_l, dma=True)
        i = step - 1
        if 0 <= i < FCH:
            deps_q = [st[(i, "l")], st_sc]
            if i >= NL:
                deps_q += [st[(i - NL, "o")], st[(i - NL, "gh")]]
            sid_q = p.stage("vector", make_q(i), deps=deps_q)
            st[(i, "q")] = sid_q
            st[(i, "o")] = p.stage("gpsimd", make_o(i), deps=[sid_q], dma=True)
            deps_s = [sid_q]
            if i >= NL:
                deps_s.append(st[(i - NL, "d2")])
            st[(i, "sig")] = p.stage("scalar", make_sig(i), deps=deps_s)
        i = step - 2
        if 0 <= i < FCH:
            deps_d = [st[(i, "sig")]]
            if i >= NL:
                deps_d.append(st[(i - NL, "gh")])
            sid_d = p.stage("vector", make_d2(i), deps=deps_d)
            st[(i, "d2")] = sid_d
            deps_g = [sid_d]
            if i >= NL:
                deps_g.append(st[(i - NL, "mx")])
            st[(i, "gh")] = p.stage("gpsimd", make_gh(i), deps=deps_g)
        i = step - 3
        if 0 <= i < FCH:
            st[(i, "mx")] = p.stage("vector", make_mx(i), deps=[st[(i, "gh")]])

    for step in range(FCH + 3):
        emit4(step)

    def s_tree(v):
        return v.tensor_reduce(accf[:], acc[:], AX.X, ALU.max)
    p.stage("vector", s_tree, deps=None)

    def s_fin(g):
        g.tensor_reduce(outs[0:1, 0:1], accf[:], AX.C, ALU.max)
        g.tensor_copy(outs[0:1, 1:2], sc[0:1, 1:2])
        return g.tensor_copy(outs[0:1, 2:3], sc[0:1, 4:5])
    p.stage("gpsimd", s_fin, deps=None)

    @dma_stage(2)
    def s_out(eng, ds):
        eng.dma_start(out=ghp, in_=outs[0:1, 0:1]).then_inc(ds, 16)
        eng.dma_start(out=scl, in_=outs[0:1, 1:3]).then_inc(ds, 16)
        return 2
    p.stage("gpsimd", s_out, deps=None, dma=True)
    p.stage("gpsimd", lambda g: g.engine_nop(), deps=None)

    p.build()
    return p


def build_phase5():
    p = Prog("p5")
    nc = p.nc
    d_up = nc.dram_tensor("d_up", [FFI, T], I16, kind="ExternalInput").ap()
    d_gt = nc.dram_tensor("d_gt", [FFI, T], I16, kind="ExternalInput").ap()
    sclin = nc.dram_tensor("sclin", [128, 10], F32, kind="ExternalInput").ap()
    d_gh = nc.dram_tensor("d_gh", [FFI, T], I16, kind="ExternalOutput").ap()
    op = nc.dram_tensor("op", [1, 1], F32, kind="ExternalOutput").ap()
    scl5 = nc.dram_tensor("scl5", [1, 1], F32, kind="ExternalOutput").ap()

    es = ExitStack()
    p._es = es
    es.__enter__()
    sb = lambda name, shape, dt: es.enter_context(nc.sbuf_tensor(name, shape, dt))

    G = 2
    dus = sb("dus", [128, 2 * G, T], I16)
    dgs = sb("dgs", [128, 2 * G, T], I16)
    dghs = sb("dghs", [128, 2 * G, T], I16)
    sig = sb("sig", [128, 2 * G, T], F32)
    d2t = sb("d2t", [128, 2 * G, T], I32)
    ght = sb("ght", [128, 2 * G, T], F32)
    ot = sb("ot", [128, 2 * G, T], F32)
    sc = sb("sc", [128, 10], F32)
    fac = sb("fac", [128, 4], F32)
    axr = sb("axr", [128, 2 * G], F32)
    acc = sb("acc", [128, 2 * G], F32)
    accf = sb("accf", [128, 1], F32)
    outs = sb("outs", [1, 2], F32)

    @dma_stage(1)
    def s_load(eng, ds):
        eng.dma_start(out=sc[:], in_=sclin).then_inc(ds, 16)
        return 1
    ld = p.stage("gpsimd", s_load, deps=[], dma=True)

    def s_scales(v):
        v.memset(acc[:], 0.0)
        v.tensor_reduce(fac[:, 2:3], sc[:, 2:10], AX.X, ALU.max)
        ins = v.tensor_scalar(fac[:, 3:4], sc[:, 1:2], 1.0 / 65536.0, None, ALU.mult)
        p.chain("vector", v, ins)
        ins = v.tensor_tensor(fac[:, 2:3], fac[:, 2:3], fac[:, 3:4], ALU.mult)
        p.chain("vector", v, ins)
        ins = v.tensor_scalar(fac[:, 0:1], fac[:, 2:3], 1.0 / QA, 1e-12, ALU.mult, ALU.max)
        p.chain("vector", v, ins)
        ins = v.reciprocal(fac[:, 1:2], fac[:, 0:1])
        p.chain("vector", v, ins)
        return v.tensor_tensor(fac[:, 1:2], fac[:, 1:2], fac[:, 3:4], ALU.mult)
    st_sc = p.stage("vector", s_scales, deps=[ld])

    NL = 2 * G

    def make_l(i):
        @dma_stage(2)
        def f(eng, ds):
            sl = i % NL
            eng.dma_start(out=dus[:, sl, :],
                          in_=d_up[i * 128:(i + 1) * 128, :]).then_inc(ds, 16)
            eng.dma_start(out=dgs[:, sl, :],
                          in_=d_gt[i * 128:(i + 1) * 128, :]).then_inc(ds, 16)
            return 2
        return f

    def make_sig(i):
        def f(s):
            sl = i % NL
            return s.activation(sig[:, sl, :], dgs[:, sl, :], AF.Sigmoid,
                                bias=0.0, scale=sc[:, 1:2])
        return f

    def make_d2(i):
        def f(v):
            sl = i % NL
            return v.tensor_scalar(d2t[:, sl, :], sig[:, sl, :], 65536.0, 65535.49,
                                   ALU.mult, ALU.min)
        return f

    def make_gh(i):
        def f(gp):
            sl = i % NL
            return gp.tensor_tensor(ght[:, sl, :], dgs[:, sl, :], d2t[:, sl, :], ALU.mult)
        return f

    def make_dgh(i):
        def f(v):
            sl = i % NL
            return v.tensor_scalar(dghs[:, sl, :], ght[:, sl, :], fac[:, 1:2], None, ALU.mult)
        return f

    def make_o(i):
        @dma_stage(1)
        def f(eng, ds):
            sl = i % NL
            eng.dma_start(out=d_gh[i * 128:(i + 1) * 128, :],
                          in_=dghs[:, sl, :]).then_inc(ds, 16)
            return 1
        return f

    def make_ot(i):
        def f(gp):
            sl = i % NL
            return gp.tensor_tensor(ot[:, sl, :], dghs[:, sl, :], dus[:, sl, :], ALU.mult)
        return f

    def make_mx(i):
        def f(v):
            sl = i % NL
            ins = v.tensor_reduce(axr[:, sl:sl + 1], ot[:, sl, :], AX.X, ALU.max,
                                  apply_absolute_value=True)
            p.chain("vector", v, ins)
            return v.tensor_tensor(acc[:, sl:sl + 1], acc[:, sl:sl + 1],
                                   axr[:, sl:sl + 1], ALU.max)
        return f

    st = {}

    def emit5(step):
        i = step
        if i < FCH:
            deps_l = [st_sc]
            if i >= NL:
                deps_l += [st[(i - NL, "sig")], st[(i - NL, "ot")]]
            st[(i, "l")] = p.stage("sync", make_l(i), deps=deps_l, dma=True)
        i = step - 1
        if 0 <= i < FCH:
            deps_s = [st[(i, "l")], st_sc]
            if i >= NL:
                deps_s.append(st[(i - NL, "d2")])
            sid_s = p.stage("scalar", make_sig(i), deps=deps_s)
            st[(i, "sig")] = sid_s
            deps_d = [sid_s]
            if i >= NL:
                deps_d.append(st[(i - NL, "gh")])
            st[(i, "d2")] = p.stage("vector", make_d2(i), deps=deps_d)
        i = step - 2
        if 0 <= i < FCH:
            deps_g = [st[(i, "d2")]]
            if i >= NL:
                deps_g.append(st[(i - NL, "dgh")])
            sid_g = p.stage("gpsimd", make_gh(i), deps=deps_g)
            st[(i, "gh")] = sid_g
            deps_h = [sid_g]
            if i >= NL:
                deps_h += [st[(i - NL, "o")], st[(i - NL, "ot")]]
            st[(i, "dgh")] = p.stage("vector", make_dgh(i), deps=deps_h)
        i = step - 3
        if 0 <= i < FCH:
            st[(i, "o")] = p.stage("gpsimd", make_o(i), deps=[st[(i, "dgh")]], dma=True)
            deps_ot = [st[(i, "dgh")]]
            if i >= NL:
                deps_ot.append(st[(i - NL, "mx")])
            sid_ot = p.stage("gpsimd", make_ot(i), deps=deps_ot)
            st[(i, "ot")] = sid_ot
            st[(i, "mx")] = p.stage("vector", make_mx(i), deps=[sid_ot])

    for step in range(FCH + 3):
        emit5(step)

    def s_tree(v):
        return v.tensor_reduce(accf[:], acc[:], AX.X, ALU.max)
    p.stage("vector", s_tree, deps=None)

    def s_fin(g):
        g.tensor_reduce(outs[0:1, 0:1], accf[:], AX.C, ALU.max)
        return g.tensor_copy(outs[0:1, 1:2], fac[0:1, 0:1])
    p.stage("gpsimd", s_fin, deps=None)

    @dma_stage(2)
    def s_out(eng, ds):
        eng.dma_start(out=op, in_=outs[0:1, 0:1]).then_inc(ds, 16)
        eng.dma_start(out=scl5, in_=outs[0:1, 1:2]).then_inc(ds, 16)
        return 2
    p.stage("gpsimd", s_out, deps=None, dma=True)
    p.stage("gpsimd", lambda g: g.engine_nop(), deps=None)

    p.build()
    return p


def build_phase6():
    p = Prog("p6")
    nc = p.nc
    d_gh = nc.dram_tensor("d_gh", [FFI, T], I16, kind="ExternalInput").ap()
    d_up = nc.dram_tensor("d_up", [FFI, T], I16, kind="ExternalInput").ap()
    x2 = nc.dram_tensor("x2", [T, D], F32, kind="ExternalInput").ap()
    w_down = nc.dram_tensor("w_down", [D, FFI], F32, kind="ExternalInput").ap()
    scl6 = nc.dram_tensor("scl6", [128, 10], F32, kind="ExternalInput").ap()
    identb = nc.dram_tensor("identb", [128, 128], BF16, kind="ExternalInput").ap()
    y = nc.dram_tensor("y", [T, D], F32, kind="ExternalOutput").ap()

    es = ExitStack()
    p._es = es
    es.__enter__()
    sb = lambda name, shape, dt: es.enter_context(nc.sbuf_tensor(name, shape, dt))
    pst = lambda name, shape, dt: es.enter_context(nc.psum_tensor(name, shape, dt))

    TH = T // 2
    oq = sb("oq", [128, FCH, TH], BF16)
    wdT = sb("wdT", [128, FCH, D], BF16)
    dghs = sb("dghs", [128, 4, TH], I16)
    dus = sb("dus", [128, 4, TH], I16)
    ot = sb("ot", [128, 4, TH], F32)
    d4t = sb("d4t", [128, 4, TH], I16)
    wstr = sb("wstr", [128, 2, FFI], F32)
    bm = sb("bm", [128, 96], F32)
    sbc = sb("sbc", [128, 96], F32)
    sbi = sb("sbi", [128, 96], F32)
    d8 = sb("d8", [128, FFI], I8)
    wqb = sb("wqb", [128, 2, FFI], BF16)
    sc = sb("sc", [128, 10], F32)
    fac = sb("fac", [128, 4], F32)
    ffo = sb("ffo", [128, 2, D], F32)
    l4s = sb("l4s", [128, 2, 8], F32)
    sqs = sb("sqs", [128, D], F32)
    xl4 = sb("xl4", [128, 2, D], F32)
    yt = sb("yt", [128, 2, D], F32)
    epsc = sb("epsc", [128, 1], F32)
    idb = sb("idb", [128, 128], BF16)

    psA = pst("psA", [128, 1024], F32)
    psB = pst("psB", [128, 1024], F32)
    PS = [psA, psB]

    @dma_stage(2)
    def s_load(eng, ds):
        eng.dma_start(out=sc[:], in_=scl6).then_inc(ds, 16)
        eng.dma_start(out=idb[:], in_=identb).then_inc(ds, 16)
        return 2
    ld = p.stage("gpsimd", s_load, deps=[], dma=True)

    def s_so(v):
        v.memset(epsc[:], EPS)
        v.tensor_reduce(fac[:, 3:4], sc[:, 2:10], AX.X, ALU.max)
        ins = v.tensor_tensor(fac[:, 2:3], sc[:, 0:1], sc[:, 1:2], ALU.mult)
        p.chain("vector", v, ins)
        ins = v.tensor_tensor(fac[:, 3:4], fac[:, 3:4], fac[:, 2:3], ALU.mult)
        p.chain("vector", v, ins)
        ins = v.tensor_scalar(fac[:, 0:1], fac[:, 3:4], 1.0 / QA, 1e-12, ALU.mult, ALU.max)
        p.chain("vector", v, ins)
        ins = v.reciprocal(fac[:, 1:2], fac[:, 0:1])
        p.chain("vector", v, ins)
        return v.tensor_tensor(fac[:, 1:2], fac[:, 1:2], fac[:, 2:3], ALU.mult)
    st_so = p.stage("vector", s_so, deps=[ld])

    def make_wl(j):
        @dma_stage(1)
        def f(eng, ds):
            eng.dma_start(out=wstr[:, j % 2, :], in_=w_down[j * 128:(j + 1) * 128, :]).then_inc(ds, 16)
            return 1
        return f

    def make_wdq(j):
        def f(v):
            sl = j % 2
            wsl = j % 2
            ins = v.tensor_reduce(bm[:], wstr[:, sl, :].rearrange("p (b q) -> p b q", q=32),
                                  AX.X, ALU.max, apply_absolute_value=True)
            p.chain("vector", v, ins)
            ins = v.tensor_scalar(sbc[:], bm[:], 1.0 / QW, 1e-12, ALU.mult, ALU.max)
            p.chain("vector", v, ins)
            ins = v.reciprocal(sbi[:], sbc[:])
            p.chain("vector", v, ins)
            ins = v.tensor_tensor(d8[:].rearrange("p (b q) -> p b q", q=32),
                                  wstr[:, sl, :].rearrange("p (b q) -> p b q", q=32),
                                  sbi[:].unsqueeze(2).broadcast_to([128, 96, 32]), ALU.mult)
            p.chain("vector", v, ins)
            return v.tensor_tensor(wqb[:, wsl, :].rearrange("p (b q) -> p b q", q=32),
                                   d8[:].rearrange("p (b q) -> p b q", q=32),
                                   sbc[:].unsqueeze(2).broadcast_to([128, 96, 32]), ALU.mult)
        return f

    def make_wtp(j, r):
        def f(te):
            ins = None
            for q in range(8):
                c = r * 8 + q
                ins = te.matmul(PS[r % 2][:, q * 128:(q + 1) * 128],
                                wqb[:, j % 2, c * 128:(c + 1) * 128], idb[:],
                                start=True, stop=True)
            return ins
        return f

    def make_wtpe(j, r):
        def f(s):
            return s.activation(
                wdT[:, r * 8:(r + 1) * 8, j * 128:(j + 1) * 128],
                PS[r % 2][:].rearrange("p (c q) -> p c q", c=8),
                AF.Identity, bias=0.0, scale=1.0)
        return f

    prev_ps = {0: None, 1: None}
    prev_dq = {}
    w_last = None
    prev_wtp = {}
    for j in range(DCH):
        deps_l = [] if j < 2 else [prev_dq[j - 2]]
        sid_l = p.stage("sync", make_wl(j), deps=deps_l, dma=True)
        deps_d = [sid_l]
        if j >= 2:
            deps_d.append(prev_wtp[j - 2])
        sid_d = p.stage("vector", make_wdq(j), deps=deps_d)
        last_e = None
        last_t = None
        for r in range(3):
            deps = [sid_d]
            if prev_ps[r % 2] is not None:
                deps.append(prev_ps[r % 2])
            sid_t = p.stage("tensor", make_wtp(j, r), deps=deps)
            sid_e = p.stage("scalar", make_wtpe(j, r), deps=[sid_t])
            prev_ps[r % 2] = sid_e
            last_e = sid_e
            last_t = sid_t
        prev_dq[j] = sid_d
        prev_wtp[j] = last_t
        w_last = last_e

    def make_ol(h, i):
        @dma_stage(2)
        def f(eng, ds):
            cs = slice(h * TH, (h + 1) * TH)
            sl = (h * FCH + i) % 4
            eng.dma_start(out=dghs[:, sl, :], in_=d_gh[i * 128:(i + 1) * 128, cs]).then_inc(ds, 16)
            eng.dma_start(out=dus[:, sl, :], in_=d_up[i * 128:(i + 1) * 128, cs]).then_inc(ds, 16)
            return 2
        return f

    def make_oprod(h, i):
        def f(gp):
            sl = (h * FCH + i) % 4
            return gp.tensor_tensor(ot[:, sl, :], dghs[:, sl, :], dus[:, sl, :], ALU.mult)
        return f

    def make_oqa(h, i):
        def f(v):
            sl = (h * FCH + i) % 4
            return v.tensor_scalar(d4t[:, sl, :], ot[:, sl, :], fac[:, 1:2], None, ALU.mult)
        return f

    def make_oqb(h, i):
        def f(v):
            sl = (h * FCH + i) % 4
            return v.tensor_scalar(oq[:, i, :], d4t[:, sl, :], fac[:, 0:1], None, ALU.mult)
        return f

    ost = {}

    def emit_ol(ii):
        h, i = divmod(ii, FCH)
        deps = []
        if ii >= 4:
            deps.append(ost[(ii - 4, "op")])
        ost[(ii, "l")] = p.stage("gpsimd", make_ol(h, i), deps=deps, dma=True)

    def emit_oprod(ii):
        h, i = divmod(ii, FCH)
        deps = [ost[(ii, "l")]]
        if ii >= 4:
            deps.append(ost[(ii - 4, "qa")])
        ost[(ii, "op")] = p.stage("gpsimd", make_oprod(h, i), deps=deps)

    def emit_oqa(ii):
        h, i = divmod(ii, FCH)
        deps = [ost[(ii, "op")], st_so]
        if ii >= 4:
            deps.append(ost[(ii - 4, "qb")])
        ost[(ii, "qa")] = p.stage("vector", make_oqa(h, i), deps=deps)

    def emit_oqb(ii, barrier):
        h, i = divmod(ii, FCH)
        deps = [ost[(ii, "qa")], st_so]
        if barrier is not None:
            deps.append(barrier)
        ost[(ii, "qb")] = p.stage("vector", make_oqb(h, i), deps=deps)

    def make_x2l(t):
        @dma_stage(1)
        def f(eng, ds):
            eng.dma_start(out=yt[:, t % 2, :], in_=x2[t * 128:(t + 1) * 128, :]).then_inc(ds, 16)
            return 1
        return f

    def make_dmm(t, ps):
        def f(te):
            ins = None
            toff = (t % (NT // 2)) * 128
            for h2 in range(2):
                for k in range(FCH):
                    ins = te.matmul(ps[:, h2 * 512:(h2 + 1) * 512],
                                    oq[:, k, toff:toff + 128],
                                    wdT[:, k, h2 * 512:(h2 + 1) * 512],
                                    start=(k == 0), stop=(k == FCH - 1))
            return ins
        return f

    def make_ev(t, ps):
        def f(s):
            return s.activation(ffo[:, t % 2, :], ps[:], AF.Identity, bias=0.0,
                                scale=1.0, accum_out=l4s[:, t % 2, 0:1])
        return f

    def make_l4a(t):
        def f(v):
            sl = t % 2
            ins = v.scalar_tensor_tensor(sqs[:], ffo[:, sl, :], 1.0, ffo[:, sl, :],
                                         ALU.bypass, ALU.mult, accum_out=l4s[:, sl, 1:2])
            p.chain("vector", v, ins)
            v.tensor_scalar(l4s[:, sl, 2:3], l4s[:, sl, 0:1], 1.0 / D, None, ALU.mult)
            ins = v.tensor_scalar(l4s[:, sl, 3:4], l4s[:, sl, 1:2], 1.0 / D, None, ALU.mult)
            p.chain("vector", v, ins)
            ins = v.scalar_tensor_tensor(l4s[:, sl, 0:1], l4s[:, sl, 2:3], 1.0, l4s[:, sl, 2:3],
                                         ALU.bypass, ALU.mult)
            p.chain("vector", v, ins)
            return v.tensor_sub(l4s[:, sl, 3:4], l4s[:, sl, 3:4], l4s[:, sl, 0:1])
        return f

    def make_l4b(t):
        def f(s):
            return s.activation(l4s[:, t % 2, 4:5], l4s[:, t % 2, 3:4], AF.Sqrt,
                                bias=epsc[:], scale=1.0)
        return f

    def make_l4c(t):
        def f(v):
            sl = t % 2
            ins = v.reciprocal(l4s[:, sl, 5:6], l4s[:, sl, 4:5])
            p.chain("vector", v, ins)
            return v.scalar_tensor_tensor(l4s[:, sl, 6:7], l4s[:, sl, 2:3], -1.0,
                                          l4s[:, sl, 5:6], ALU.mult, ALU.mult)
        return f

    def make_l4d(t):
        def f(s):
            sl = t % 2
            return s.activation(xl4[:, sl, :], ffo[:, sl, :], AF.Identity,
                                bias=l4s[:, sl, 6:7], scale=l4s[:, sl, 5:6])
        return f

    def make_y(t):
        def f(v):
            sl = t % 2
            return v.tensor_tensor(yt[:, sl, :], yt[:, sl, :], xl4[:, sl, :], ALU.add)
        return f

    def make_yo(t):
        @dma_stage(1)
        def f(eng, ds):
            eng.dma_start(out=y[t * 128:(t + 1) * 128, :], in_=yt[:, t % 2, :]).then_inc(ds, 16)
            return 1
        return f

    prev_t = {}

    def front6(t, oq_last):
        deps_x = [] if t < 2 else [prev_t[(t - 2, "y")]]
        sid_xl = p.stage("sync", make_x2l(t), deps=deps_x, dma=True)
        ps = PS[t % 2]
        deps_m = [oq_last, w_last]
        if t >= 2:
            deps_m.append(prev_t[(t - 2, "ev")])
        sid_m = p.stage("tensor", make_dmm(t, ps), deps=deps_m)
        deps_e = [sid_m]
        if t >= 2:
            deps_e.append(prev_t[(t - 2, "l4d")])
        sid_e = p.stage("scalar", make_ev(t, ps), deps=deps_e)
        sid_a = p.stage("vector", make_l4a(t), deps=[sid_e])
        sid_b = p.stage("scalar", make_l4b(t), deps=[sid_a])
        prev_t[(t, "ev")] = sid_e
        prev_t[(t, "mm")] = sid_m
        prev_t[(t, "l4b")] = sid_b
        prev_t[(t, "xl")] = sid_xl

    def back6(t):
        sid_c = p.stage("vector", make_l4c(t), deps=[prev_t[(t, "l4b")]])
        sid_d = p.stage("scalar", make_l4d(t), deps=[sid_c])
        sid_y = p.stage("vector", make_y(t), deps=[sid_d, prev_t[(t, "xl")]])
        sid_o = p.stage("gpsimd", make_yo(t), deps=[sid_y], dma=True)
        prev_t[(t, "y")] = sid_o
        prev_t[(t, "l4d")] = sid_d

    for s in range(FCH + 2):
        if s < FCH:
            emit_ol(s)
        if 1 <= s <= FCH:
            emit_oprod(s - 1)
            emit_oqa(s - 1)
        if 2 <= s <= FCH + 1:
            emit_oqb(s - 2, None)
    oq_last0 = ost[(FCH - 1, "qb")]
    for ii in range(FCH, FCH + 4):
        emit_ol(ii)
    for ii in range(FCH, FCH + 2):
        emit_oprod(ii)
        emit_oqa(ii)
    for tt in range(NT // 2 + 1):
        if tt < NT // 2:
            front6(tt, oq_last0)
        if tt >= 1:
            back6(tt - 1)
    barrier = prev_t[(7, "mm")]
    for s in range(FCH):
        if FCH + 4 + s < 2 * FCH:
            emit_ol(FCH + 4 + s)
        if FCH + 2 + s < 2 * FCH:
            emit_oprod(FCH + 2 + s)
            emit_oqa(FCH + 2 + s)
        emit_oqb(FCH + s, barrier)
    oq_last1 = ost[(2 * FCH - 1, "qb")]
    for tt in range(NT // 2 + 1):
        t = NT // 2 + tt
        if tt < NT // 2:
            front6(t, oq_last1)
        if tt >= 1:
            back6(t - 1)
    p.stage("gpsimd", lambda g: g.engine_nop(), deps=None)

    p.build()
    return p



_PHASES = {}


def _get_phase(name, builder):
    if name not in _PHASES:
        _PHASES[name] = builder()
    return _PHASES[name]


def _shard_x(x):
    out = []
    for c in range(NCORES):
        b, m = divmod(c, 4)
        out.append(np.ascontiguousarray(x[b, m * T:(m + 1) * T, :]))
    return out


def kernel(**inputs):
    x = np.asarray(inputs["L0_attn_x"], dtype=np.float32)
    ident_b = np.eye(128, dtype=np.float32).astype(BF16NP)
    ident_f = np.eye(128, dtype=np.float32)
    f32 = lambda k: np.asarray(inputs[k], np.float32)

    shards = _shard_x(x)
    cores = list(range(NCORES))

    p1 = _get_phase("p1", build_phase1)
    im1 = [{
        "x": shards[c],
        "w_zf": f32("L0_attn_W_zf"), "w_hf": f32("L0_attn_W_hf"),
        "w_zs": f32("L0_attn_W_zs"), "w_hs": f32("L0_attn_W_hs"),
        "b_zf": f32("L0_attn_b_zf").reshape(1, DF),
        "b_hf": f32("L0_attn_b_hf").reshape(1, DF),
        "b_zs": f32("L0_attn_b_zs").reshape(1, DF),
        "b_hs": f32("L0_attn_b_hs").reshape(1, DF),
        "identb": ident_b,
    } for c in cores]
    r1 = run_bass_kernel_spmd(p1.nc, im1, core_ids=cores).results

    carrP = np.stack([np.asarray(r1[c]["carr"])[0] for c in cores])
    carrH = np.stack([np.asarray(r1[c]["carr"])[1] for c in cores])
    masks = []
    for c in cores:
        m = np.zeros((128, 7), np.float32)
        for j in range(7):
            if j // 4 == c // 4 and j < c:
                m[:, j] = 1.0
        masks.append(m)

    p2 = _get_phase("p2", build_phase2)
    im2 = [{
        "x": shards[c],
        "h_in": np.asarray(r1[c]["h_out"]),
        "a_in": np.asarray(r1[c]["a_out"]),
        "carrP": carrP, "carrH": carrH,
        "prevmask": masks[c],
        "w_g": f32("L0_attn_W_g"),
        "b_g": f32("L0_attn_b_g").reshape(1, D),
        "identb": ident_b, "identf": ident_f,
    } for c in cores]
    r2 = run_bass_kernel_spmd(p2.nc, im2, core_ids=cores).results

    sxall = np.zeros((128, 8), np.float32)
    for c in cores:
        sxall[:, c] = np.asarray(r2[c]["sxp"])[0, 0]

    p3 = _get_phase("p3", build_phase3)
    im3 = []
    for c in cores:
        im3.append({
            "x2": np.asarray(r2[c]["x2"]),
            "stats3t": np.asarray(r2[c]["stats3"]),
            "sxall": sxall,
            "w_up": f32("L0_ffn_w_up"), "w_gate": f32("L0_ffn_w_gate"),
            "identb": ident_b,
        })
    r3 = run_bass_kernel_spmd(p3.nc, im3, core_ids=cores).results

    mxall = np.zeros((128, 16), np.float32)
    for c in cores:
        mxall[:, c] = np.asarray(r3[c]["mxp"])[0, 0]
        mxall[:, 8 + c] = np.asarray(r3[c]["mxp"])[0, 1]

    p4 = _get_phase("p4", build_phase4)
    im4 = [{
        "up_in": np.asarray(r3[c]["up_out"]),
        "gt_in": np.asarray(r3[c]["gt_out"]),
        "mxall": mxall,
    } for c in cores]
    r4 = run_bass_kernel_spmd(p4.nc, im4, core_ids=cores).results

    sclin5 = np.zeros((128, 10), np.float32)
    sclin5[:, 0] = np.asarray(r4[0]["scl"])[0, 0]
    sclin5[:, 1] = np.asarray(r4[0]["scl"])[0, 1]
    for c in cores:
        sclin5[:, 2 + c] = np.asarray(r4[c]["ghp"])[0, 0]

    p5 = _get_phase("p5", build_phase5)
    im5 = [{
        "d_up": np.asarray(r4[c]["d_up"]),
        "d_gt": np.asarray(r4[c]["d_gt"]),
        "sclin": sclin5,
    } for c in cores]
    r5 = run_bass_kernel_spmd(p5.nc, im5, core_ids=cores).results

    scl6 = np.zeros((128, 10), np.float32)
    scl6[:, 0] = np.asarray(r4[0]["scl"])[0, 0]
    scl6[:, 1] = np.asarray(r5[0]["scl5"])[0, 0]
    for c in cores:
        scl6[:, 2 + c] = np.asarray(r5[c]["op"])[0, 0]

    p6 = _get_phase("p6", build_phase6)
    im6 = [{
        "d_gh": np.asarray(r5[c]["d_gh"]),
        "d_up": np.asarray(r4[c]["d_up"]),
        "x2": np.asarray(r2[c]["x2"]),
        "w_down": f32("L0_ffn_w_down"),
        "scl6": scl6,
        "identb": ident_b,
    } for c in cores]
    r6 = run_bass_kernel_spmd(p6.nc, im6, core_ids=cores).results

    out = np.empty((B, S, D), np.float32)
    for c in cores:
        b_, m_ = divmod(c, 4)
        out[b_, m_ * T:(m_ + 1) * T, :] = np.asarray(r6[c]["y"])
    return out
